# version 05
# baseline (speedup 1.0000x reference)
from contextlib import ExitStack

import numpy as np
import concourse.bass as bass
import concourse.mybir as mybir
from concourse.bass_utils import run_bass_kernel_spmd

F32 = mybir.dt.float32
I32 = mybir.dt.int32
U8 = mybir.dt.uint8
AF = mybir.ActivationFunctionType
ALU = mybir.AluOpType
AX = mybir.AxisListType

D = 2048
TOK = 1024
NCH = 8
KC = 16
HD = 16
SEQ = 2048
MLP = 16384
NSUP = 64
DCH = 32
ENGINES = ("tensor", "vector", "scalar", "gpsimd", "sync")


class Prog:

    def __init__(self, sems):
        self.sems = sems
        self.cnt = {k: 0 for k in sems}
        self.streams = {e: [] for e in ENGINES}

    def on(self, eng, f):
        self.streams[eng].append(("op", f))

    def inc(self, eng, f, sem, k=1):
        self.cnt[sem] += k
        v = self.cnt[sem]
        self.streams[eng].append(("inc", f, sem, k))
        return v

    def wait(self, eng, sem, v):
        if v <= 0:
            return
        assert v <= self.cnt[sem], f"wait {sem}>={v} but only {self.cnt[sem]} emitted"
        self.streams[eng].append(("wait", sem, v))

    def bar(self):
        for e in ENGINES:
            self.inc(e, lambda eng: eng.nop(), "BAR", 1)
        t = self.cnt["BAR"]
        for e in ENGINES:
            self.wait(e, "BAR", t)

    def replay(self, eng_name, eng):
        for item in self.streams[eng_name]:
            if item[0] == "op":
                item[1](eng)
            elif item[0] == "inc":
                item[1](eng).then_inc(self.sems[item[2]], item[3])
            else:
                eng.wait_ge(self.sems[item[1]], item[2])


def band_ap(t, row0, nrows_per_band, nbands, col0, ncols, row_stride):
    return bass.AP(
        t,
        row0 * row_stride + col0,
        [[row_stride, nrows_per_band], [row_stride * nrows_per_band, nbands], [1, ncols]],
    )


def build_nc():
    nc = bass.Bass("TRN2", target_bir_lowering=False)

    x_own = nc.dram_tensor("x_own", [TOK, D], F32, kind="ExternalInput")
    wqt = nc.dram_tensor("wqt", [D, D], F32, kind="ExternalInput")
    wkt = nc.dram_tensor("wkt", [D, D], F32, kind="ExternalInput")
    wvt = nc.dram_tensor("wvt", [D, D], F32, kind="ExternalInput")
    wot = nc.dram_tensor("wot", [D, D], F32, kind="ExternalInput")
    bq_col = nc.dram_tensor("bq_col", [D, 1], F32, kind="ExternalInput")
    bo_rep = nc.dram_tensor("bo_rep", [128, D], F32, kind="ExternalInput")
    bk_rep = nc.dram_tensor("bk_rep", [128, D], F32, kind="ExternalInput")
    bv_rep = nc.dram_tensor("bv_rep", [128, D], F32, kind="ExternalInput")
    decay_col = nc.dram_tensor("decay_col", [TOK, 1], F32, kind="ExternalInput")
    decay_rep = nc.dram_tensor("decay_rep", [128, TOK], F32, kind="ExternalInput")
    ident_in = nc.dram_tensor("ident_in", [128, 128], F32, kind="ExternalInput")
    g_pkt = nc.dram_tensor("g_pkt", [D // 2, MLP], U8, kind="ExternalInput")
    u_pkt = nc.dram_tensor("u_pkt", [D // 2, MLP], U8, kind="ExternalInput")
    d_pkt = nc.dram_tensor("d_pkt", [MLP // 2, D], U8, kind="ExternalInput")
    g_scr = nc.dram_tensor("g_scr", [D // 2, MLP], F32, kind="ExternalInput")
    u_scr = nc.dram_tensor("u_scr", [D // 2, MLP], F32, kind="ExternalInput")
    d_scr = nc.dram_tensor("d_scr", [MLP // 2, D], F32, kind="ExternalInput")
    y_out = nc.dram_tensor("y", [TOK, D], F32, kind="ExternalOutput")

    k_scr = nc.dram_tensor("k_scr", [128, NCH * D], F32, kind="Internal")
    v_scr = nc.dram_tensor("v_scr", [128, NCH * D], F32, kind="Internal")
    x2_scr = nc.dram_tensor("x2_scr", [128, NCH * D], F32, kind="Internal")
    it_scr = nc.dram_tensor("it_scr", [128, 2 * 128 * 512], F32, kind="Internal")
    cc_in = nc.dram_tensor("cc_in", [128, 2064], F32, kind="Internal")
    cc_out = nc.dram_tensor("cc_out", [128, 2064], F32, kind="Internal")

    es_root = ExitStack()
    sems = {n: es_root.enter_context(nc.semaphore(n))
            for n in ("BAR", "SD", "GD", "TT", "VV", "SS", "GG", "CC")}
    p = Prog(sems)

    A1 = nc.alloc_sbuf_tensor("A1", [128, 16384], F32)
    A2 = nc.alloc_sbuf_tensor("A2", [128, 16384], F32)
    ident = nc.alloc_sbuf_tensor("ident", [128, 128], F32)
    cst = nc.alloc_sbuf_tensor("cst", [128, 8], F32)
    ones1 = nc.alloc_sbuf_tensor("ones1", [1, 128], F32)
    onesc = nc.alloc_sbuf_tensor("onesc", [128, 1], F32)
    bq_sb = nc.alloc_sbuf_tensor("bq_sb", [128, 16], F32)
    dcol_sb = nc.alloc_sbuf_tensor("dcol_sb", [128, 8], F32)
    stat = nc.alloc_sbuf_tensor("stat", [128, 96], F32)

    block_ctx = es_root.enter_context(nc.Block())

    p.inc("sync", lambda e: e.dma_start(out=ident.ap(), in_=ident_in.ap()), "SD", 16)
    p.inc("sync", lambda e: e.dma_start(
        out=bq_sb.ap(), in_=bass.AP(bq_col, 0, [[1, 128], [128, 16], [1, 1]]),
        allow_slow_non_contiguous=True), "SD", 16)
    p0sd = p.inc("sync", lambda e: e.dma_start(
        out=dcol_sb.ap(), in_=bass.AP(decay_col, 0, [[1, 128], [128, 8], [1, 1]]),
        allow_slow_non_contiguous=True), "SD", 16)
    p.on("vector", lambda e: e.memset(cst[:, 0:1], 7.5))
    p.on("vector", lambda e: e.memset(cst[:, 1:2], -8.0))
    p.on("vector", lambda e: e.memset(cst[:, 2:3], -10.0))
    p.on("vector", lambda e: e.memset(cst[:, 3:4], 1e-5))
    p.on("vector", lambda e: e.memset(cst[:, 4:5], 1e-6))
    p.on("vector", lambda e: e.memset(ones1.ap(), 1.0))
    p.inc("vector", lambda e: e.memset(onesc.ap(), 1.0), "VV", 1)
    p.bar()

    def emit_ln(src_fn, dst_fn, c0, dump):
        p.on("vector", lambda e: e.reduce_sum(
            stat[:, c0:c0 + 1], src_fn(), axis=AX.X, negate=True))
        v1 = p.inc("vector", lambda e: e.memset(stat[:, 88:89], 0.0), "VV", 1)
        p.wait("scalar", "VV", v1)
        s2 = p.inc("scalar", lambda e: e.mul(stat[:, c0 + 1:c0 + 2], stat[:, c0:c0 + 1], 1.0 / D), "SS", 1)
        p.wait("vector", "SS", s2)
        v2 = p.inc("vector", lambda e: e.tensor_scalar_add(
            stat[:, c0 + 3:c0 + 4], stat[:, c0 + 1:c0 + 2], 0.0), "VV", 1)
        p.wait("scalar", "VV", v2)
        p.on("scalar", lambda e, dump=dump: e.activation(
            dump[:, :], src_fn(), AF.Square, bias=stat[:, c0 + 1:c0 + 2],
            accum_out=stat[:, c0 + 2:c0 + 3]))
        s3 = p.inc("scalar", lambda e: e.copy(stat[:, 89:90], stat[:, 88:89]), "SS", 1)
        p.wait("vector", "SS", s3)
        v3 = p.inc("vector", lambda e: e.tensor_scalar(
            stat[:, c0 + 4:c0 + 5], stat[:, c0 + 2:c0 + 3], 1.0 / D, 1e-5,
            ALU.mult, ALU.add), "VV", 1)
        p.wait("scalar", "VV", v3)
        s4 = p.inc("scalar", lambda e: e.activation(
            stat[:, c0 + 5:c0 + 6], stat[:, c0 + 4:c0 + 5], AF.Sqrt), "SS", 1)
        p.wait("vector", "SS", s4)
        p.on("vector", lambda e: e.reciprocal(
            stat[:, c0 + 6:c0 + 7], stat[:, c0 + 5:c0 + 6]))
        v4 = p.inc("vector", lambda e: e.memset(stat[:, 90:91], 0.0), "VV", 1)
        p.wait("scalar", "VV", v4)
        s5 = p.inc("scalar", lambda e: e.activation(
            stat[:, c0 + 8:c0 + 9], stat[:, c0 + 6:c0 + 7], AF.Square), "SS", 1)
        p.wait("vector", "SS", s5)
        p.on("vector", lambda e: e.scalar_tensor_tensor(
            stat[:, c0 + 9:c0 + 10], stat[:, c0 + 8:c0 + 9], -0.5, stat[:, c0 + 4:c0 + 5],
            ALU.mult, ALU.mult))
        v5 = p.inc("vector", lambda e: e.tensor_scalar_mul(
            stat[:, c0 + 7:c0 + 8], stat[:, c0 + 6:c0 + 7], 1.5), "VV", 1)
        p.wait("scalar", "VV", v5)
        s6 = p.inc("scalar", lambda e: e.activation(
            stat[:, c0 + 5:c0 + 6], stat[:, c0 + 9:c0 + 10], AF.Identity,
            bias=stat[:, c0 + 7:c0 + 8], scale=stat[:, c0 + 6:c0 + 7]), "SS", 1)
        p.wait("vector", "SS", s6)
        v6 = p.inc("vector", lambda e: e.tensor_mul(
            stat[:, c0 + 3:c0 + 4], stat[:, c0 + 5:c0 + 6], stat[:, c0 + 1:c0 + 2]), "VV", 1)
        p.wait("scalar", "VV", v6)
        s7 = p.inc("scalar", lambda e: e.activation(
            dst_fn(), src_fn(), AF.Identity, bias=stat[:, c0 + 3:c0 + 4],
            scale=stat[:, c0 + 5:c0 + 6]), "SS", 1)
        return s7

    es1 = ExitStack()
    xch = [es1.enter_context(nc.sbuf_tensor(f"xch{i}", [128, D], F32)) for i in range(2)]
    ln_dump1 = es1.enter_context(nc.sbuf_tensor("ln_dump1", [128, D], F32))
    xln1 = es1.enter_context(nc.sbuf_tensor("xln1", [128, D], F32))
    tr_ps1 = [es1.enter_context(nc.psum_tensor(f"trps{i}", [128, 512], F32)) for i in range(2)]
    xln_free = 0
    trev = [0, 0]
    xch_free = [0, 0]
    p.wait("tensor", "SD", p0sd)
    for tc in range(NCH):
        bi = tc % 2
        p.wait("sync", "SS", xch_free[bi])
        ldx = p.inc("sync", lambda e, tc=tc, bi=bi: e.dma_start(
            out=xch[bi].ap(), in_=x_own.ap()[tc * 128:(tc + 1) * 128, :]), "SD", 16)
        p.wait("vector", "SD", ldx)
        p.wait("scalar", "SD", ldx)
        p.wait("scalar", "TT", xln_free)
        sdone = emit_ln(lambda bi=bi: xch[bi][:, :], lambda: xln1[:, :], (tc % 8) * 10, ln_dump1)
        xch_free[bi] = sdone
        for g in range(4):
            pi = g % 2
            p.wait("tensor", "SS", sdone)
            p.wait("tensor", "VV", trev[pi])
            for j in range(4):
                kt = g * 4 + j
                fn = lambda e, kt=kt, j=j, pi=pi: e.transpose(
                    tr_ps1[pi][:, j * 128:(j + 1) * 128], xln1[:, kt * 128:(kt + 1) * 128], ident.ap())
                if j < 3:
                    p.on("tensor", fn)
                else:
                    tdone = p.inc("tensor", fn, "TT", 1)
            if g == 3:
                xln_free = tdone
            p.wait("vector", "TT", tdone)
            for j in range(4):
                fn = lambda e, g=g, tc=tc, pi=pi, j=j: e.tensor_copy(
                    A1[:, (g * 4 + j) * TOK + tc * 128: (g * 4 + j) * TOK + (tc + 1) * 128],
                    tr_ps1[pi][:, j * 128:(j + 1) * 128])
                if j < 3:
                    p.on("vector", fn)
                else:
                    trev[pi] = p.inc("vector", fn, "VV", 1)
    es1.close()
    p.bar()

    es2 = ExitStack()
    wsl = es2.enter_context(nc.sbuf_tensor("wsl", [128, KC * 512], F32))
    brep = es2.enter_context(nc.sbuf_tensor("brep", [128, D], F32))
    pj_ps = [es2.enter_context(nc.psum_tensor(f"pjps{i}", [128, 512], F32)) for i in range(2)]
    ev1 = [es2.enter_context(nc.sbuf_tensor(f"ev1{i}", [128, 512], F32)) for i in range(2)]
    ev2 = [es2.enter_context(nc.sbuf_tensor(f"ev2{i}", [128, 512], F32)) for i in range(2)]
    ps_ev = [0, 0]
    ev1_free = [0, 0]
    ev2_free = [0, 0]
    p.wait("scalar", "SD", p0sd)

    def proj_tok_major(wt_dram, scr_dram, is_k, bias_rep_dram):
        p.wait("sync", "VV", p.cnt["VV"])
        ldb = p.inc("sync", lambda e: e.dma_start(out=brep.ap(), in_=bias_rep_dram.ap()), "SD", 16)
        for ds in range(4):
            p.wait("sync", "TT", p.cnt["TT"])
            ldw = p.inc("sync", lambda e, ds=ds: e.dma_start(
                out=wsl.ap(), in_=band_ap(wt_dram, 0, 128, KC, ds * 512, 512, D)), "SD", 16)
            for tc in range(NCH):
                pi = tc % 2
                p.wait("tensor", "SD", ldw)
                p.wait("tensor", "VV", ps_ev[pi])
                for kc in range(KC):
                    fn = lambda e, kc=kc, tc=tc, pi=pi: e.matmul(
                        pj_ps[pi][:, :],
                        A1[:, kc * TOK + tc * 128: kc * TOK + (tc + 1) * 128],
                        wsl[:, kc * 512:(kc + 1) * 512],
                        start=(kc == 0), stop=(kc == KC - 1))
                    if kc < KC - 1:
                        p.on("tensor", fn)
                    else:
                        mm = p.inc("tensor", fn, "TT", 1)
                p.wait("vector", "TT", mm)
                p.wait("vector", "SD", ldb)
                p.wait("vector", "SS" if is_k else "GD", ev1_free[pi])
                vv = p.inc("vector", lambda e, ds=ds, pi=pi: e.tensor_add(
                    ev1[pi][:, :], pj_ps[pi][:, :], brep[:, ds * 512:(ds + 1) * 512]), "VV", 1)
                ps_ev[pi] = vv
                if is_k:
                    p.wait("scalar", "VV", vv)
                    p.wait("scalar", "GD", ev2_free[pi])
                    ss = p.inc("scalar", lambda e, tc=tc, pi=pi: e.activation(
                        ev2[pi][:, :], ev1[pi][:, :], AF.Relu,
                        scale=dcol_sb[:, tc:tc + 1]), "SS", 1)
                    ev1_free[pi] = ss
                    src = ev2
                    p.wait("gpsimd", "SS", ss)
                else:
                    src = ev1
                    p.wait("gpsimd", "VV", vv)
                st = p.inc("gpsimd", lambda e, src=src, pi=pi, tc=tc, ds=ds: e.dma_start(
                    out=bass.AP(scr_dram, tc * D + ds * 512, [[NCH * D, 128], [1, 512]]),
                    in_=src[pi][:, :]), "GD", 16)
                if is_k:
                    ev2_free[pi] = st
                else:
                    ev1_free[pi] = st

    proj_tok_major(wkt, k_scr, True, bk_rep)
    proj_tok_major(wvt, v_scr, False, bv_rep)
    p.wait("gpsimd", "GD", p.cnt["GD"])
    es2.close()
    p.bar()

    es_kv = ExitStack()
    kv2_sb = es_kv.enter_context(nc.sbuf_tensor("kv2_sb", [128, D], F32))
    ks2_sb = es_kv.enter_context(nc.sbuf_tensor("ks2_sb", [128, 16], F32))
    es3 = ExitStack()
    kch = [es3.enter_context(nc.sbuf_tensor(f"kch{i}", [128, D], F32)) for i in range(2)]
    vch = [es3.enter_context(nc.sbuf_tensor(f"vch{i}", [128, D], F32)) for i in range(2)]
    kv_sb = es3.enter_context(nc.sbuf_tensor("kv_sb", [128, D], F32))
    ks_sb = es3.enter_context(nc.sbuf_tensor("ks_sb", [128, 16], F32))
    kv_ps = [es3.enter_context(nc.psum_tensor(f"kvps{i}", [128, 512], F32)) for i in range(4)]
    ksm_ps = es3.enter_context(nc.psum_tensor("ksmps", [128, 16], F32))
    kv_ev = [0, 0, 0, 0]
    ksm_ev = 0
    kv_ld = [0, 0]

    for tc in range(NCH):
        bi = tc % 2
        p.wait("gpsimd", "TT", kv_ld[bi])
        p.inc("gpsimd", lambda e, tc=tc, bi=bi: e.dma_start(
            out=kch[bi].ap(), in_=bass.AP(k_scr, tc * D, [[NCH * D, 128], [1, D]])), "GD", 16)
        lv = p.inc("gpsimd", lambda e, tc=tc, bi=bi: e.dma_start(
            out=vch[bi].ap(), in_=bass.AP(v_scr, tc * D, [[NCH * D, 128], [1, D]])), "GD", 16)
        p.wait("tensor", "GD", lv)
        p.wait("tensor", "VV", ksm_ev)
        for hgrp in range(4):
            p.wait("tensor", "VV", kv_ev[hgrp])
            for hh in range(4):
                h = hgrp * 4 + hh
                fn = lambda e, h=h, hh=hh, hgrp=hgrp, bi=bi: e.matmul(
                    kv_ps[hgrp][:, hh * 128:(hh + 1) * 128],
                    kch[bi][:, h * 128:(h + 1) * 128],
                    vch[bi][:, h * 128:(h + 1) * 128],
                    start=(hh == 0), stop=(hh == 3), skip_group_check=True)
                if hh < 3:
                    p.on("tensor", fn)
                else:
                    mm = p.inc("tensor", fn, "TT", 1)
            p.wait("vector", "TT", mm)
            if tc == 0:
                kv_ev[hgrp] = p.inc("vector", lambda e, hgrp=hgrp: e.tensor_copy(
                    kv_sb[:, hgrp * 512:(hgrp + 1) * 512], kv_ps[hgrp][:, :]), "VV", 1)
            else:
                kv_ev[hgrp] = p.inc("vector", lambda e, hgrp=hgrp: e.tensor_add(
                    kv_sb[:, hgrp * 512:(hgrp + 1) * 512],
                    kv_sb[:, hgrp * 512:(hgrp + 1) * 512], kv_ps[hgrp][:, :]), "VV", 1)
        for h in range(HD):
            fn = lambda e, h=h, bi=bi: e.matmul(
                ksm_ps[:, h:h + 1], kch[bi][:, h * 128:(h + 1) * 128], onesc.ap(),
                start=(h == 0), stop=(h == HD - 1), skip_group_check=True)
            if h < HD - 1:
                p.on("tensor", fn)
            else:
                mmk = p.inc("tensor", fn, "TT", 1)
        kv_ld[bi] = mmk
        p.wait("vector", "TT", mmk)
        if tc == 0:
            ksm_ev = p.inc("vector", lambda e: e.tensor_copy(ks_sb[:, :], ksm_ps[:, :]), "VV", 1)
        else:
            ksm_ev = p.inc("vector", lambda e: e.tensor_add(ks_sb[:, :], ks_sb[:, :], ksm_ps[:, :]), "VV", 1)
    p.wait("gpsimd", "VV", max(max(kv_ev), ksm_ev))
    p.inc("gpsimd", lambda e: e.dma_start(
        out=bass.AP(cc_in, 0, [[2064, 128], [1, D]]), in_=kv_sb.ap()), "GD", 16)
    ccs = p.inc("gpsimd", lambda e: e.dma_start(
        out=bass.AP(cc_in, D, [[2064, 128], [1, 16]]), in_=ks_sb.ap()), "GD", 16)
    es3.close()
    p.bar()

    p.wait("gpsimd", "GD", ccs)
    ccd = p.inc("gpsimd", lambda e: e.collective_compute(
        "AllReduce", ALU.add,
        replica_groups=[[0, 1], [2, 3], [4, 5], [6, 7]],
        ins=[cc_in.ap().opt()], outs=[cc_out.ap().opt()]), "CC", 1)
    p.wait("gpsimd", "CC", ccd)
    p.inc("gpsimd", lambda e: e.dma_start(
        out=kv2_sb.ap(), in_=bass.AP(cc_out, 0, [[2064, 128], [1, D]])), "GD", 16)
    kvr = p.inc("gpsimd", lambda e: e.dma_start(
        out=ks2_sb.ap(), in_=bass.AP(cc_out, D, [[2064, 128], [1, 16]])), "GD", 16)

    es4 = ExitStack()
    wsl4 = es4.enter_context(nc.sbuf_tensor("wsl4", [128, KC * 512], F32))
    drep_sb = es4.enter_context(nc.sbuf_tensor("drep_sb", [128, TOK], F32))
    q_ps = [es4.enter_context(nc.psum_tensor(f"qps{i}", [128, 512], F32)) for i in range(2)]
    tq4 = [es4.enter_context(nc.sbuf_tensor(f"tq4{i}", [128, 512], F32)) for i in range(2)]
    ldd = p.inc("sync", lambda e: e.dma_start(out=drep_sb.ap(), in_=decay_rep.ap()), "SD", 16)
    p.wait("scalar", "SD", p0sd)
    p.wait("vector", "SD", ldd)
    qev = [0, 0]
    tq4_free = [0, 0]
    for ds in range(4):
        p.wait("sync", "TT", p.cnt["TT"])
        ldw = p.inc("sync", lambda e, ds=ds: e.dma_start(
            out=wsl4.ap(), in_=band_ap(wqt, 0, 128, KC, ds * 512, 512, D)), "SD", 16)
        for db in range(4):
            for th in range(2):
                pi = (db * 2 + th) % 2
                p.wait("tensor", "SD", ldw)
                p.wait("tensor", "SS", qev[pi])
                for kc in range(KC):
                    fn = lambda e, kc=kc, db=db, th=th, pi=pi: e.matmul(
                        q_ps[pi][:, :],
                        wsl4[:, kc * 512 + db * 128: kc * 512 + (db + 1) * 128],
                        A1[:, kc * TOK + th * 512: kc * TOK + (th + 1) * 512],
                        start=(kc == 0), stop=(kc == KC - 1))
                    if kc < KC - 1:
                        p.on("tensor", fn)
                    else:
                        mm = p.inc("tensor", fn, "TT", 1)
                p.wait("scalar", "TT", mm)
                p.wait("scalar", "VV", tq4_free[pi])
                qev[pi] = ss = p.inc("scalar", lambda e, ds=ds, db=db, pi=pi: e.activation(
                    tq4[pi][:, :], q_ps[pi][:, :], AF.Relu,
                    bias=bq_sb[:, ds * 4 + db: ds * 4 + db + 1]), "SS", 1)
                p.wait("vector", "SS", ss)
                tq4_free[pi] = p.inc("vector", lambda e, ds=ds, db=db, th=th, pi=pi: e.tensor_mul(
                    A2[:, (ds * 4 + db) * TOK + th * 512: (ds * 4 + db) * TOK + (th + 1) * 512],
                    tq4[pi][:, :], drep_sb[:, th * 512:(th + 1) * 512]), "VV", 1)
    es4.close()
    p.bar()

    es5 = ExitStack()
    out_ps = [es5.enter_context(nc.psum_tensor(f"ops{i}", [128, 512], F32)) for i in range(2)]
    dn_ps = es5.enter_context(nc.psum_tensor("dnps", [128, 512], F32))
    zb_ps = es5.enter_context(nc.psum_tensor("zbps", [128, 512], F32))
    zrow = es5.enter_context(nc.sbuf_tensor("zrow", [1, 512], F32))
    zrow2 = es5.enter_context(nc.sbuf_tensor("zrow2", [1, 512], F32))
    zb_sb = [es5.enter_context(nc.sbuf_tensor(f"zbsb{i}", [128, 512], F32)) for i in range(2)]
    o_ev = [0, 0]
    dn_ev_s = 0
    z_used = 0
    z_used_v = 0
    zb_free = [0, 0]
    p.wait("tensor", "GD", kvr)
    for h in range(HD):
        for th in range(2):
            pi = th
            p.wait("tensor", "VV", o_ev[pi])
            p.inc("tensor", lambda e, h=h, th=th, pi=pi: e.matmul(
                out_ps[pi][:, :], kv2_sb[:, h * 128:(h + 1) * 128],
                A2[:, h * TOK + th * 512: h * TOK + (th + 1) * 512],
                start=True, stop=True), "TT", 1)
            p.wait("tensor", "SS", dn_ev_s)
            md = p.inc("tensor", lambda e, h=h, th=th: e.matmul(
                dn_ps[0:1, :], ks2_sb[:, h:h + 1],
                A2[:, h * TOK + th * 512: h * TOK + (th + 1) * 512],
                start=True, stop=True), "TT", 1)
            p.wait("scalar", "TT", md)
            p.wait("scalar", "VV", z_used_v)
            sz = p.inc("scalar", lambda e: e.activation(
                zrow[0:1, :], dn_ps[0:1, :], AF.Identity, bias=cst[0:1, 4:5]), "SS", 1)
            dn_ev_s = sz
            p.wait("vector", "SS", sz)
            p.wait("vector", "TT", z_used)
            p.on("vector", lambda e: e.reciprocal(zrow2[0:1, :], zrow[0:1, :]))
            zv = p.inc("vector", lambda e: e.memset(stat[:, 91:92], 0.0), "VV", 1)
            p.wait("tensor", "VV", zv)
            p.wait("tensor", "VV", zb_free[pi])
            z_used = mz = p.inc("tensor", lambda e: e.matmul(
                zb_ps[:, :], ones1.ap(), zrow2[0:1, :], start=True, stop=True), "TT", 1)
            z_used_v = zv
            p.wait("vector", "TT", mz)
            p.inc("vector", lambda e, pi=pi: e.tensor_copy(zb_sb[pi][:, :], zb_ps[:, :]), "VV", 1)
            o_ev[pi] = zb_free[pi] = p.inc("vector", lambda e, h=h, th=th, pi=pi: e.tensor_mul(
                A1[:, h * TOK + th * 512: h * TOK + (th + 1) * 512],
                out_ps[pi][:, :], zb_sb[pi][:, :]), "VV", 1)
    es5.close()
    es_kv.close()
    p.bar()

    es6 = ExitStack()
    wsl6 = es6.enter_context(nc.sbuf_tensor("wsl6", [128, KC * 512], F32))
    brep6 = es6.enter_context(nc.sbuf_tensor("brep6", [128, D], F32))
    xsl = [es6.enter_context(nc.sbuf_tensor(f"xsl{i}", [128, 512], F32)) for i in range(2)]
    o_psb = [es6.enter_context(nc.psum_tensor(f"opsb{i}", [128, 512], F32)) for i in range(2)]
    tt6 = [es6.enter_context(nc.sbuf_tensor(f"tt6{i}", [128, 512], F32)) for i in range(2)]
    ldb6 = p.inc("sync", lambda e: e.dma_start(out=brep6.ap(), in_=bo_rep.ap()), "SD", 16)
    oev6 = [0, 0]
    tt6_free = [0, 0]
    xsl_free = [0, 0]
    for ds in range(4):
        p.wait("sync", "TT", p.cnt["TT"])
        ldw = p.inc("sync", lambda e, ds=ds: e.dma_start(
            out=wsl6.ap(), in_=band_ap(wot, 0, 128, KC, ds * 512, 512, D)), "SD", 16)
        for tc in range(NCH):
            pi = tc % 2
            p.wait("sync", "VV", xsl_free[pi])
            lx = p.inc("sync", lambda e, tc=tc, ds=ds, pi=pi: e.dma_start(
                out=xsl[pi].ap(),
                in_=bass.AP(x_own, tc * 128 * D + ds * 512, [[D, 128], [1, 512]])), "SD", 16)
            p.wait("tensor", "SD", ldw)
            p.wait("tensor", "VV", oev6[pi])
            for ec in range(KC):
                fn = lambda e, ec=ec, tc=tc, pi=pi: e.matmul(
                    o_psb[pi][:, :],
                    A1[:, ec * TOK + tc * 128: ec * TOK + (tc + 1) * 128],
                    wsl6[:, ec * 512:(ec + 1) * 512],
                    start=(ec == 0), stop=(ec == KC - 1))
                if ec < KC - 1:
                    p.on("tensor", fn)
                else:
                    mm = p.inc("tensor", fn, "TT", 1)
            p.wait("vector", "TT", mm)
            p.wait("vector", "SD", ldb6)
            p.wait("vector", "VV", tt6_free[pi])
            oev6[pi] = p.inc("vector", lambda e, ds=ds, pi=pi: e.tensor_add(
                tt6[pi][:, :], o_psb[pi][:, :], brep6[:, ds * 512:(ds + 1) * 512]), "VV", 1)
            p.wait("vector", "SD", lx)
            v2 = p.inc("vector", lambda e, tc=tc, ds=ds, pi=pi: e.tensor_add(
                A2[:, tc * D + ds * 512: tc * D + (ds + 1) * 512], tt6[pi][:, :], xsl[pi][:, :]), "VV", 1)
            tt6_free[pi] = v2
            xsl_free[pi] = v2
    es6.close()
    p.bar()

    es6b = ExitStack()
    ln_dump2 = es6b.enter_context(nc.sbuf_tensor("ln_dump2", [128, D], F32))
    xln2 = es6b.enter_context(nc.sbuf_tensor("xln2", [128, D], F32))
    tr_ps2 = [es6b.enter_context(nc.psum_tensor(f"trps2{i}", [128, 512], F32)) for i in range(2)]
    xln2_free = 0
    trev = [0, 0]
    x2_spill = 0
    for tc in range(NCH):
        p.wait("scalar", "TT", xln2_free)
        sdone = emit_ln(lambda tc=tc: A2[:, tc * D:(tc + 1) * D], lambda: xln2[:, :], (tc % 8) * 10, ln_dump2)
        x2_spill = p.inc("gpsimd", lambda e, tc=tc: e.dma_start(
            out=bass.AP(x2_scr, tc * D, [[NCH * D, 128], [1, D]]),
            in_=A2[:, tc * D:(tc + 1) * D]), "GD", 16)
        for g in range(4):
            pi = g % 2
            p.wait("tensor", "SS", sdone)
            p.wait("tensor", "VV", trev[pi])
            for j in range(4):
                band = g * 4 + j
                s8, par = band // 2, band % 2
                fn = lambda e, s8=s8, par=par, j=j, pi=pi: e.transpose(
                    tr_ps2[pi][:, j * 128:(j + 1) * 128],
                    xln2[:, 256 * s8 + par: 256 * s8 + 256: 2], ident.ap())
                if j < 3:
                    p.on("tensor", fn)
                else:
                    tdone = p.inc("tensor", fn, "TT", 1)
            if g == 3:
                xln2_free = tdone
            p.wait("vector", "TT", tdone)
            for j in range(4):
                fn = lambda e, g=g, tc=tc, pi=pi, j=j: e.tensor_copy(
                    A1[:, (g * 4 + j) * TOK + tc * 128: (g * 4 + j) * TOK + (tc + 1) * 128],
                    tr_ps2[pi][:, j * 128:(j + 1) * 128])
                if j < 3:
                    p.on("vector", fn)
                else:
                    trev[pi] = p.inc("vector", fn, "VV", 1)
    p.wait("gpsimd", "GD", p.cnt["GD"])
    es6b.close()
    p.bar()

    def emit_decode(dec, bi, stage_ap, w, scale_lo_ap, scale_hi_ap, out_lo_ap, out_hi_ap,
                    stage_gd, wdec_free_tt, scale_sd):
        ni, c_f, sgn, mp, ta, tb = (dec["ni"][bi], dec["c_f"][bi], dec["sgn"][bi],
                                    dec["mp"][bi], dec["ta"][bi], dec["tb"][bi])
        w2 = 2 * w
        p.wait("vector", "GD", stage_gd)
        p.on("vector", lambda e: e.tensor_single_scalar(ni[:, 0:w], stage_ap, 15, ALU.bitwise_and))
        p.on("vector", lambda e: e.tensor_single_scalar(
            ni[:, w:w2], stage_ap, 4, ALU.logical_shift_right))
        p.wait("vector", "SS", dec["cf_free"][bi])
        v1 = p.inc("vector", lambda e: e.tensor_copy(c_f[:, 0:w2], ni[:, 0:w2]), "VV", 1)
        dec["ni_free"][bi] = v1
        p.wait("scalar", "VV", v1)
        s1 = p.inc("scalar", lambda e: e.activation(
            sgn[:, 0:w2], c_f[:, 0:w2], AF.Sign, bias=cst[:, 0:1], scale=-1.0), "SS", 1)
        dec["cf_free"][bi] = s1
        p.wait("vector", "SS", s1)
        v2 = p.inc("vector", lambda e: e.scalar_tensor_tensor(
            mp[:, 0:w2], sgn[:, 0:w2], 4.0, c_f[:, 0:w2], ALU.mult, ALU.add), "VV", 1)
        p.wait("scalar", "VV", v2)
        p.on("scalar", lambda e: e.activation(ta[:, 0:w2], mp[:, 0:w2], AF.Relu, bias=cst[:, 1:2]))
        s2 = p.inc("scalar", lambda e: e.activation(
            tb[:, 0:w2], mp[:, 0:w2], AF.Relu, bias=cst[:, 2:3]), "SS", 1)
        p.wait("vector", "SS", s2)
        p.on("vector", lambda e: e.scalar_tensor_tensor(
            ta[:, 0:w2], ta[:, 0:w2], 0.5, tb[:, 0:w2], ALU.mult, ALU.add))
        p.on("vector", lambda e: e.scalar_tensor_tensor(
            mp[:, 0:w2], mp[:, 0:w2], 0.5, ta[:, 0:w2], ALU.mult, ALU.add))
        p.on("vector", lambda e: e.tensor_mul(tb[:, 0:w2], mp[:, 0:w2], sgn[:, 0:w2]))
        p.on("vector", lambda e: e.scalar_tensor_tensor(
            c_f[:, 0:w2], sgn[:, 0:w2], -2.0, tb[:, 0:w2], ALU.mult, ALU.add))
        p.wait("vector", "TT", wdec_free_tt)
        p.wait("vector", "SD", scale_sd)
        p.on("vector", lambda e: e.tensor_mul(out_lo_ap, c_f[:, 0:w], scale_lo_ap))
        v4 = p.inc("vector", lambda e: e.tensor_mul(out_hi_ap, c_f[:, w:w2], scale_hi_ap), "VV", 1)
        return v4

    def mk_dec(es, w, tag):
        d = {"ni": [es.enter_context(nc.sbuf_tensor(f"dni{tag}{i}", [128, 2 * w], I32)) for i in range(2)]}
        for nm in ("c_f", "sgn", "mp", "ta", "tb"):
            d[nm] = [es.enter_context(nc.sbuf_tensor(f"d{nm}{tag}{i}", [128, 2 * w], F32)) for i in range(2)]
        for nm in ("ni_free", "cf_free", "sgn_free"):
            d[nm] = [0, 0]
        return d

    es7 = ExitStack()
    stg7 = [es7.enter_context(nc.sbuf_tensor(f"stg7{i}", [128, 2048], I32)) for i in range(2)]
    it7 = [es7.enter_context(nc.sbuf_tensor(f"it7{i}", [128, 2048], F32)) for i in range(2)]
    dec7 = mk_dec(es7, 256, "a")
    gu_ps = [es7.enter_context(nc.psum_tensor(f"gups{i}", [128, 512], F32)) for i in range(2)]
    WDc = [0, 4096]
    SCc = [10240, 12288]
    GEL = 8192
    wdec_last_mm = [0, 0]
    scale_free = [0, 0]
    stg_free = [0, 0]
    gu_ev = [0, 0]
    gu_ev_sem = ["SS", "SS"]
    gel_free = 0
    it7_free = [0, 0]
    p.wait("sync", "GD", x2_spill)
    p.wait("vector", "GD", x2_spill)

    for sup in range(NSUP):
        for mi, (pkt, scr) in enumerate(((g_pkt, g_scr), (u_pkt, u_scr))):
            wb = mi
            p.wait("sync", "VV", scale_free[wb])
            lds = p.inc("sync", lambda e, scr=scr, sup=sup, wb=wb: e.dma_start(
                out=bass.AP(A2, SCc[wb], [[16384, 128], [1, 2048]]),
                in_=band_ap(scr, 0, 128, 8, sup * 256, 256, MLP)), "SD", 16)
            p.wait("gpsimd", "VV", stg_free[wb])
            ldp = p.inc("gpsimd", lambda e, pkt=pkt, sup=sup, wb=wb: e.dma_start(
                out=stg7[wb].ap(), in_=band_ap(pkt, 0, 128, 8, sup * 256, 256, MLP)), "GD", 16)
            for j in range(8):
                v4 = emit_decode(
                    dec7, j % 2,
                    stg7[wb][:, j * 256:(j + 1) * 256], 256,
                    bass.AP(A2, SCc[wb] + j * 256, [[16384, 128], [1, 256]]),
                    bass.AP(A2, SCc[wb] + j * 256, [[16384, 128], [1, 256]]),
                    bass.AP(A2, WDc[wb] + (2 * j) * 256, [[16384, 128], [1, 256]]),
                    bass.AP(A2, WDc[wb] + (2 * j + 1) * 256, [[16384, 128], [1, 256]]),
                    ldp, wdec_last_mm[wb] if j == 0 else 0, lds)
            stg_free[wb] = p.cnt["VV"]
            scale_free[wb] = v4
            p.wait("tensor", "VV", v4)
            for nt in range(2):
                for th in range(2):
                    pi = (nt * 2 + th) % 2
                    p.wait("tensor", gu_ev_sem[pi], gu_ev[pi])
                    for kb in range(KC):
                        fn = lambda e, kb=kb, nt=nt, th=th, pi=pi, wb=wb: e.matmul(
                            gu_ps[pi][:, :],
                            bass.AP(A2, WDc[wb] + kb * 256 + nt, [[16384, 128], [2, 128]]),
                            A1[:, kb * TOK + th * 512: kb * TOK + (th + 1) * 512],
                            start=(kb == 0), stop=(kb == KC - 1))
                        if kb < KC - 1:
                            p.on("tensor", fn)
                        else:
                            mm = p.inc("tensor", fn, "TT", 1)
                    if nt == 1 and th == 1:
                        wdec_last_mm[wb] = mm
                    if mi == 0:
                        p.wait("scalar", "TT", mm)
                        if nt == 0 and th == 0:
                            p.wait("scalar", "VV", gel_free)
                        gu_ev[pi] = p.inc("scalar", lambda e, nt=nt, th=th, pi=pi: e.activation(
                            bass.AP(A2, GEL + nt * 1024 + th * 512, [[16384, 128], [1, 512]]),
                            gu_ps[pi][:, :], AF.Gelu_apprx_tanh), "SS", 1)
                        gu_ev_sem[pi] = "SS"
                    else:
                        bi7 = sup % 2
                        p.wait("vector", "TT", mm)
                        p.wait("vector", "SS", p.cnt["SS"])
                        if nt == 0 and th == 0:
                            p.wait("vector", "GD", it7_free[bi7])
                        vv = p.inc("vector", lambda e, nt=nt, th=th, pi=pi, bi7=bi7: e.tensor_mul(
                            it7[bi7][:, nt * 1024 + th * 512: nt * 1024 + (th + 1) * 512],
                            gu_ps[pi][:, :],
                            bass.AP(A2, GEL + nt * 1024 + th * 512, [[16384, 128], [1, 512]])), "VV", 1)
                        gu_ev[pi] = vv
                        gu_ev_sem[pi] = "VV"
                        if nt == 1 and th == 1:
                            gel_free = vv
                            p.wait("gpsimd", "VV", vv)
                            for th2 in range(2):
                                it7_free[bi7] = p.inc(
                                    "gpsimd", lambda e, sup=sup, bi7=bi7, th2=th2: e.dma_start(
                                        out=bass.AP(it_scr, th2 * 65536 + 2 * sup * 512,
                                                    [[2 * 128 * 512, 128], [512, 2], [1, 512]]),
                                        in_=bass.AP(it7[bi7], th2 * 512,
                                                    [[2048, 128], [1024, 2], [1, 512]])),
                                    "GD", 16)
    p.wait("gpsimd", "GD", p.cnt["GD"])
    es7.close()
    p.bar()

    es8 = ExitStack()
    stg8 = es8.enter_context(nc.sbuf_tensor("stg8", [128, 2048], I32))
    scl8 = es8.enter_context(nc.sbuf_tensor("scl8", [128, 2048], F32))
    it8 = [es8.enter_context(nc.sbuf_tensor(f"it8{i}", [128, 2048], F32)) for i in range(2)]
    dec8 = mk_dec(es8, 256, "b")
    dn_ps8 = [es8.enter_context(nc.psum_tensor(f"dps8{i}", [128, 512], F32)) for i in range(2)]
    WD8 = [0, 8192]
    wd8_last_mm = [0, 0]
    stg8_free = 0
    scl8_free = 0
    acc_ev = [0, 0]
    it8_free = [0, 0]

    for ch in range(DCH):
        wb = ch % 2
        for jt in range(2):
            p.wait("gpsimd", "VV", stg8_free)
            ldp = p.inc("gpsimd", lambda e, ch=ch, jt=jt: e.dma_start(
                out=stg8.ap(), in_=band_ap(d_pkt, 256 * ch + 128 * jt, 128, 1, 0, 2048, D)), "GD", 16)
            p.wait("sync", "VV", scl8_free)
            lds = p.inc("sync", lambda e, ch=ch, jt=jt: e.dma_start(
                out=scl8.ap(), in_=band_ap(d_scr, 256 * ch + 128 * jt, 128, 1, 0, 2048, D)), "SD", 16)
            for hs in range(8):
                v4 = emit_decode(
                    dec8, hs % 2,
                    stg8[:, hs * 256:(hs + 1) * 256], 256,
                    scl8[:, hs * 256:(hs + 1) * 256],
                    scl8[:, hs * 256:(hs + 1) * 256],
                    bass.AP(A1, WD8[wb] + (2 * jt) * 2048 + hs * 256, [[16384, 128], [1, 256]]),
                    bass.AP(A1, WD8[wb] + (2 * jt + 1) * 2048 + hs * 256, [[16384, 128], [1, 256]]),
                    ldp, wd8_last_mm[wb] if (jt == 0 and hs == 0) else 0, lds)
            stg8_free = p.cnt["VV"]
            scl8_free = v4
        p.wait("tensor", "VV", v4)
        for th in range(2):
            bi8 = th
            p.wait("gpsimd", "TT", it8_free[bi8])
            ldi = p.inc("gpsimd", lambda e, ch=ch, th=th, bi8=bi8: e.dma_start(
                out=it8[bi8].ap(),
                in_=bass.AP(it_scr, th * 65536 + 4 * ch * 512, [[2 * 128 * 512, 128], [1, 2048]])),
                "GD", 16)
            for hb in range(16):
                pi = hb % 2
                p.wait("tensor", "GD", ldi)
                p.wait("tensor", "VV", acc_ev[pi])
                for g4 in range(4):
                    fn = lambda e, g4=g4, hb=hb, pi=pi, wb=wb, bi8=bi8: e.matmul(
                        dn_ps8[pi][:, :],
                        bass.AP(A1, WD8[wb] + g4 * 2048 + hb * 128, [[16384, 128], [1, 128]]),
                        it8[bi8][:, g4 * 512:(g4 + 1) * 512],
                        start=(g4 == 0), stop=(g4 == 3))
                    if g4 < 3:
                        p.on("tensor", fn)
                    else:
                        mm = p.inc("tensor", fn, "TT", 1)
                if hb == 15:
                    it8_free[bi8] = mm
                    if th == 1:
                        wd8_last_mm[wb] = mm
                p.wait("vector", "TT", mm)
                if ch == 0:
                    acc_ev[pi] = p.inc("vector", lambda e, hb=hb, th=th, pi=pi: e.tensor_copy(
                        A2[:, hb * TOK + th * 512: hb * TOK + (th + 1) * 512], dn_ps8[pi][:, :]), "VV", 1)
                else:
                    acc_ev[pi] = p.inc("vector", lambda e, hb=hb, th=th, pi=pi: e.tensor_add(
                        A2[:, hb * TOK + th * 512: hb * TOK + (th + 1) * 512],
                        A2[:, hb * TOK + th * 512: hb * TOK + (th + 1) * 512], dn_ps8[pi][:, :]), "VV", 1)
    es8.close()
    p.bar()

    es9 = ExitStack()
    x2c = [es9.enter_context(nc.sbuf_tensor(f"x2c{i}", [128, D], F32)) for i in range(2)]
    y_sb = [es9.enter_context(nc.sbuf_tensor(f"ysb{i}", [128, D], F32)) for i in range(2)]
    tr9 = es9.enter_context(nc.psum_tensor("tr9", [128, 2048], F32))
    x2c_free = [0, 0]
    y_free = [0, 0]
    tr9_free = 0
    for tc in range(NCH):
        bi = tc % 2
        p.wait("gpsimd", "VV", x2c_free[bi])
        lx2 = p.inc("gpsimd", lambda e, tc=tc, bi=bi: e.dma_start(
            out=x2c[bi].ap(), in_=bass.AP(x2_scr, tc * D, [[NCH * D, 128], [1, D]])), "GD", 16)
        p.wait("tensor", "VV", tr9_free)
        for ht in range(16):
            fn = lambda e, ht=ht, tc=tc: e.transpose(
                tr9[:, ht * 128:(ht + 1) * 128],
                A2[:, ht * TOK + tc * 128: ht * TOK + (tc + 1) * 128], ident.ap())
            if ht < 15:
                p.on("tensor", fn)
            else:
                tdone = p.inc("tensor", fn, "TT", 1)
        p.wait("vector", "TT", tdone)
        p.wait("vector", "GD", lx2)
        p.wait("vector", "GD", y_free[bi])
        for q in range(4):
            fn = lambda e, bi=bi, q=q: e.tensor_add(
                y_sb[bi][:, q * 512:(q + 1) * 512], tr9[:, q * 512:(q + 1) * 512],
                x2c[bi][:, q * 512:(q + 1) * 512])
            if q < 3:
                p.on("vector", fn)
            else:
                tr9_free = vv = p.inc("vector", fn, "VV", 1)
        x2c_free[bi] = vv
        p.wait("gpsimd", "VV", vv)
        y_free[bi] = p.inc("gpsimd", lambda e, tc=tc, bi=bi: e.dma_start(
            out=y_out.ap()[tc * 128:(tc + 1) * 128, :], in_=y_sb[bi][:, :]), "GD", 16)
    p.wait("gpsimd", "GD", p.cnt["GD"])
    es9.close()

    @block_ctx.tensor
    def _(eng):
        p.replay("tensor", eng)

    @block_ctx.vector
    def _(eng):
        p.replay("vector", eng)

    @block_ctx.scalar
    def _(eng):
        p.replay("scalar", eng)

    @block_ctx.gpsimd
    def _(eng):
        p.replay("gpsimd", eng)

    @block_ctx.sync
    def _(eng):
        p.replay("sync", eng)

    es_root.close()
    return nc


def kernel(**inputs):
    x = np.asarray(inputs["L0_attn_x"], dtype=np.float32)
    assert x.shape == (4, SEQ, D)

    def bytes_of(a):
        a = np.asarray(a)
        return np.ascontiguousarray(a.view(np.uint8).reshape(a.shape[0], a.shape[1], 4)[:, :, 0])

    wq_t = np.ascontiguousarray(np.asarray(inputs["L0_attn_Wq"], dtype=np.float32).T)
    wk_t = np.ascontiguousarray(np.asarray(inputs["L0_attn_Wk"], dtype=np.float32).T)
    wv_t = np.ascontiguousarray(np.asarray(inputs["L0_attn_Wv"], dtype=np.float32).T)
    wo_t = np.ascontiguousarray(np.asarray(inputs["L0_attn_Wo"], dtype=np.float32).T)
    bq = np.ascontiguousarray(np.asarray(inputs["L0_attn_bq"], dtype=np.float32).reshape(D, 1))
    bk = np.ascontiguousarray(np.broadcast_to(np.asarray(inputs["L0_attn_bk"], dtype=np.float32), (128, D)))
    bv = np.ascontiguousarray(np.broadcast_to(np.asarray(inputs["L0_attn_bv"], dtype=np.float32), (128, D)))
    bo = np.ascontiguousarray(np.broadcast_to(np.asarray(inputs["L0_attn_bo"], dtype=np.float32), (128, D)))
    g_pk = np.ascontiguousarray(bytes_of(inputs["L0_ffn_gate_W_packed"]).T)
    u_pk = np.ascontiguousarray(bytes_of(inputs["L0_ffn_up_W_packed"]).T)
    d_pk = np.ascontiguousarray(bytes_of(inputs["L0_ffn_down_W_packed"]).T)
    g_sc = np.ascontiguousarray(np.repeat(np.asarray(inputs["L0_ffn_gate_scales"], dtype=np.float32).T, 16, axis=0))
    u_sc = np.ascontiguousarray(np.repeat(np.asarray(inputs["L0_ffn_up_scales"], dtype=np.float32).T, 16, axis=0))
    d_sc = np.ascontiguousarray(np.repeat(np.asarray(inputs["L0_ffn_down_scales"], dtype=np.float32).T, 16, axis=0))
    ident = np.eye(128, dtype=np.float32)
    pos = np.arange(SEQ, dtype=np.float64)
    decay = np.exp(-pos / float(SEQ)).astype(np.float32)

    nc = build_nc()
    in_maps = []
    for c in range(8):
        b, hf = c // 2, c % 2
        dcol = np.ascontiguousarray(decay[hf * TOK:(hf + 1) * TOK].reshape(TOK, 1))
        drep = np.ascontiguousarray(np.broadcast_to(decay[hf * TOK:(hf + 1) * TOK], (128, TOK)))
        in_maps.append({
            "x_own": np.ascontiguousarray(x[b, hf * TOK:(hf + 1) * TOK, :]),
            "wqt": wq_t, "wkt": wk_t, "wvt": wv_t, "wot": wo_t,
            "bq_col": bq, "bk_rep": bk, "bv_rep": bv, "bo_rep": bo,
            "decay_col": dcol, "decay_rep": drep, "ident_in": ident,
            "g_pkt": g_pk, "u_pkt": u_pk, "d_pkt": d_pk,
            "g_scr": g_sc, "u_scr": u_sc, "d_scr": d_sc,
        })
    res = run_bass_kernel_spmd(nc, in_maps, core_ids=list(range(8)))
    y = np.empty((4, SEQ, D), dtype=np.float32)
    for c in range(8):
        b, hf = c // 2, c % 2
        y[b, hf * TOK:(hf + 1) * TOK, :] = res.results[c]["y"]
    return y


# version 10 (final)
# speedup vs baseline: 1.2378x; 1.2378x over previous
from contextlib import ExitStack

import numpy as np
import concourse.bass as bass
import concourse.mybir as mybir
from concourse.bass_utils import run_bass_kernel_spmd

F32 = mybir.dt.float32
I32 = mybir.dt.int32
U8 = mybir.dt.uint8
AF = mybir.ActivationFunctionType
ALU = mybir.AluOpType
AX = mybir.AxisListType

D = 2048
TOK = 1024
NCH = 8
KC = 16
HD = 16
SEQ = 2048
MLP = 16384
NSUP = 64
DCH = 32
ENGINES = ("tensor", "vector", "scalar", "gpsimd", "sync")


class Prog:

    def __init__(self, sems):
        self.sems = sems
        self.cnt = {k: 0 for k in sems}
        self.streams = {e: [] for e in ENGINES}

    def on(self, eng, f):
        self.streams[eng].append(("op", f))

    def inc(self, eng, f, sem, k=1):
        self.cnt[sem] += k
        v = self.cnt[sem]
        self.streams[eng].append(("inc", f, sem, k))
        return v

    def wait(self, eng, sem, v):
        if v <= 0:
            return
        assert v <= self.cnt[sem], f"wait {sem}>={v} but only {self.cnt[sem]} emitted"
        self.streams[eng].append(("wait", sem, v))

    def bar(self):
        for e in ENGINES:
            self.inc(e, lambda eng: eng.nop(), "BAR", 1)
        t = self.cnt["BAR"]
        for e in ENGINES:
            self.wait(e, "BAR", t)

    def replay(self, eng_name, eng):
        for item in self.streams[eng_name]:
            if item[0] == "op":
                item[1](eng)
            elif item[0] == "inc":
                item[1](eng).then_inc(self.sems[item[2]], item[3])
            else:
                eng.wait_ge(self.sems[item[1]], item[2])


def band_ap(t, row0, nrows_per_band, nbands, col0, ncols, row_stride):
    return bass.AP(
        t,
        row0 * row_stride + col0,
        [[row_stride, nrows_per_band], [row_stride * nrows_per_band, nbands], [1, ncols]],
    )


def build_nc():
    nc = bass.Bass("TRN2", target_bir_lowering=False)

    x_own = nc.dram_tensor("x_own", [TOK, D], F32, kind="ExternalInput")
    wqt = nc.dram_tensor("wqt", [D, D], F32, kind="ExternalInput")
    wkt = nc.dram_tensor("wkt", [D, D], F32, kind="ExternalInput")
    wvt = nc.dram_tensor("wvt", [D, D], F32, kind="ExternalInput")
    wot = nc.dram_tensor("wot", [D, D], F32, kind="ExternalInput")
    bq_col = nc.dram_tensor("bq_col", [D, 1], F32, kind="ExternalInput")
    bo_rep = nc.dram_tensor("bo_rep", [128, D], F32, kind="ExternalInput")
    bk_rep = nc.dram_tensor("bk_rep", [128, D], F32, kind="ExternalInput")
    bv_rep = nc.dram_tensor("bv_rep", [128, D], F32, kind="ExternalInput")
    decay_col = nc.dram_tensor("decay_col", [TOK, 1], F32, kind="ExternalInput")
    decay_rep = nc.dram_tensor("decay_rep", [128, TOK], F32, kind="ExternalInput")
    ident_in = nc.dram_tensor("ident_in", [128, 128], F32, kind="ExternalInput")
    g_pkt = nc.dram_tensor("g_pkt", [D // 2, MLP], U8, kind="ExternalInput")
    u_pkt = nc.dram_tensor("u_pkt", [D // 2, MLP], U8, kind="ExternalInput")
    d_pkt = nc.dram_tensor("d_pkt", [MLP // 2, D], U8, kind="ExternalInput")
    g_scr = nc.dram_tensor("g_scr", [D // 2, MLP], F32, kind="ExternalInput")
    u_scr = nc.dram_tensor("u_scr", [D // 2, MLP], F32, kind="ExternalInput")
    d_scr = nc.dram_tensor("d_scr", [MLP // 2, D], F32, kind="ExternalInput")
    y_out = nc.dram_tensor("y", [TOK, D], F32, kind="ExternalOutput")

    k_scr = nc.dram_tensor("k_scr", [128, NCH * D], F32, kind="Internal")
    v_scr = nc.dram_tensor("v_scr", [128, NCH * D], F32, kind="Internal")
    x2_scr = nc.dram_tensor("x2_scr", [128, NCH * D], F32, kind="Internal")
    it_scr = nc.dram_tensor("it_scr", [128, 2 * 128 * 512], F32, kind="Internal")
    cc_in = nc.dram_tensor("cc_in", [128, 2064], F32, kind="Internal")
    cc_out = nc.dram_tensor("cc_out", [128, 2064], F32, kind="Internal")

    es_root = ExitStack()
    sems = {n: es_root.enter_context(nc.semaphore(n))
            for n in ("BAR", "SD", "GD", "TT", "VV", "SS", "GG", "CC")}
    p = Prog(sems)

    A1 = nc.alloc_sbuf_tensor("A1", [128, 16384], F32)
    A2 = nc.alloc_sbuf_tensor("A2", [128, 16384], F32)
    ident = nc.alloc_sbuf_tensor("ident", [128, 128], F32)
    cst = nc.alloc_sbuf_tensor("cst", [128, 8], F32)
    ones1 = nc.alloc_sbuf_tensor("ones1", [1, 128], F32)
    onesc = nc.alloc_sbuf_tensor("onesc", [128, 1], F32)
    bq_sb = nc.alloc_sbuf_tensor("bq_sb", [128, 16], F32)
    dcol_sb = nc.alloc_sbuf_tensor("dcol_sb", [128, 8], F32)
    stat = nc.alloc_sbuf_tensor("stat", [128, 96], F32)

    block_ctx = es_root.enter_context(nc.Block())

    p.inc("sync", lambda e: e.dma_start(out=ident.ap(), in_=ident_in.ap()), "SD", 16)
    p.inc("sync", lambda e: e.dma_start(
        out=bq_sb.ap(), in_=bass.AP(bq_col, 0, [[1, 128], [128, 16], [1, 1]]),
        allow_slow_non_contiguous=True), "SD", 16)
    p0sd = p.inc("sync", lambda e: e.dma_start(
        out=dcol_sb.ap(), in_=bass.AP(decay_col, 0, [[1, 128], [128, 8], [1, 1]]),
        allow_slow_non_contiguous=True), "SD", 16)
    p.on("vector", lambda e: e.memset(cst[:, 0:1], 7.5))
    p.on("vector", lambda e: e.memset(cst[:, 1:2], -8.0))
    p.on("vector", lambda e: e.memset(cst[:, 2:3], -10.0))
    p.on("vector", lambda e: e.memset(cst[:, 3:4], 1e-5))
    p.on("vector", lambda e: e.memset(cst[:, 4:5], 1e-6))
    p.on("vector", lambda e: e.memset(ones1.ap(), 1.0))
    p.inc("vector", lambda e: e.memset(onesc.ap(), 1.0), "VV", 1)
    p.bar()

    def emit_ln(src_fn, dst_fn, c0, dump):
        p.on("vector", lambda e: e.reduce_sum(
            stat[:, c0:c0 + 1], src_fn(), axis=AX.X, negate=True))
        v1 = p.inc("vector", lambda e: e.memset(stat[:, 88:89], 0.0), "VV", 1)
        p.wait("scalar", "VV", v1)
        s2 = p.inc("scalar", lambda e: e.mul(stat[:, c0 + 1:c0 + 2], stat[:, c0:c0 + 1], 1.0 / D), "SS", 1)
        p.wait("vector", "SS", s2)
        v2 = p.inc("vector", lambda e: e.tensor_scalar_add(
            stat[:, c0 + 3:c0 + 4], stat[:, c0 + 1:c0 + 2], 0.0), "VV", 1)
        p.wait("scalar", "VV", v2)
        p.on("scalar", lambda e, dump=dump: e.activation(
            dump[:, :], src_fn(), AF.Square, bias=stat[:, c0 + 1:c0 + 2],
            accum_out=stat[:, c0 + 2:c0 + 3]))
        s3 = p.inc("scalar", lambda e: e.copy(stat[:, 89:90], stat[:, 88:89]), "SS", 1)
        p.wait("vector", "SS", s3)
        v3 = p.inc("vector", lambda e: e.tensor_scalar(
            stat[:, c0 + 4:c0 + 5], stat[:, c0 + 2:c0 + 3], 1.0 / D, 1e-5,
            ALU.mult, ALU.add), "VV", 1)
        p.wait("scalar", "VV", v3)
        s4 = p.inc("scalar", lambda e: e.activation(
            stat[:, c0 + 5:c0 + 6], stat[:, c0 + 4:c0 + 5], AF.Sqrt), "SS", 1)
        p.wait("vector", "SS", s4)
        p.on("vector", lambda e: e.reciprocal(
            stat[:, c0 + 6:c0 + 7], stat[:, c0 + 5:c0 + 6]))
        v4 = p.inc("vector", lambda e: e.memset(stat[:, 90:91], 0.0), "VV", 1)
        p.wait("scalar", "VV", v4)
        s5 = p.inc("scalar", lambda e: e.activation(
            stat[:, c0 + 8:c0 + 9], stat[:, c0 + 6:c0 + 7], AF.Square), "SS", 1)
        p.wait("vector", "SS", s5)
        p.on("vector", lambda e: e.scalar_tensor_tensor(
            stat[:, c0 + 9:c0 + 10], stat[:, c0 + 8:c0 + 9], -0.5, stat[:, c0 + 4:c0 + 5],
            ALU.mult, ALU.mult))
        v5 = p.inc("vector", lambda e: e.tensor_scalar_mul(
            stat[:, c0 + 7:c0 + 8], stat[:, c0 + 6:c0 + 7], 1.5), "VV", 1)
        p.wait("scalar", "VV", v5)
        s6 = p.inc("scalar", lambda e: e.activation(
            stat[:, c0 + 5:c0 + 6], stat[:, c0 + 9:c0 + 10], AF.Identity,
            bias=stat[:, c0 + 7:c0 + 8], scale=stat[:, c0 + 6:c0 + 7]), "SS", 1)
        p.wait("vector", "SS", s6)
        v6 = p.inc("vector", lambda e: e.tensor_mul(
            stat[:, c0 + 3:c0 + 4], stat[:, c0 + 5:c0 + 6], stat[:, c0 + 1:c0 + 2]), "VV", 1)
        p.wait("scalar", "VV", v6)
        s7 = p.inc("scalar", lambda e: e.activation(
            dst_fn(), src_fn(), AF.Identity, bias=stat[:, c0 + 3:c0 + 4],
            scale=stat[:, c0 + 5:c0 + 6]), "SS", 1)
        return s7

    es1 = ExitStack()
    xch = [es1.enter_context(nc.sbuf_tensor(f"xch{i}", [128, D], F32)) for i in range(2)]
    ln_dump1 = es1.enter_context(nc.sbuf_tensor("ln_dump1", [128, D], F32))
    xln1 = es1.enter_context(nc.sbuf_tensor("xln1", [128, D], F32))
    tr_ps1 = [es1.enter_context(nc.psum_tensor(f"trps{i}", [128, 512], F32)) for i in range(2)]
    xln_free = 0
    trev = [0, 0]
    xch_free = [0, 0]
    p.wait("tensor", "SD", p0sd)
    for tc in range(NCH):
        bi = tc % 2
        p.wait("sync", "SS", xch_free[bi])
        ldx = p.inc("sync", lambda e, tc=tc, bi=bi: e.dma_start(
            out=xch[bi].ap(), in_=x_own.ap()[tc * 128:(tc + 1) * 128, :]), "SD", 16)
        p.wait("vector", "SD", ldx)
        p.wait("scalar", "SD", ldx)
        p.wait("scalar", "TT", xln_free)
        sdone = emit_ln(lambda bi=bi: xch[bi][:, :], lambda: xln1[:, :], (tc % 8) * 10, ln_dump1)
        xch_free[bi] = sdone
        for g in range(4):
            pi = g % 2
            p.wait("tensor", "SS", sdone)
            p.wait("tensor", "VV", trev[pi])
            for j in range(4):
                kt = g * 4 + j
                fn = lambda e, kt=kt, j=j, pi=pi: e.transpose(
                    tr_ps1[pi][:, j * 128:(j + 1) * 128], xln1[:, kt * 128:(kt + 1) * 128], ident.ap())
                if j < 3:
                    p.on("tensor", fn)
                else:
                    tdone = p.inc("tensor", fn, "TT", 1)
            if g == 3:
                xln_free = tdone
            p.wait("vector", "TT", tdone)
            for j in range(4):
                fn = lambda e, g=g, tc=tc, pi=pi, j=j: e.tensor_copy(
                    A1[:, (g * 4 + j) * TOK + tc * 128: (g * 4 + j) * TOK + (tc + 1) * 128],
                    tr_ps1[pi][:, j * 128:(j + 1) * 128])
                if j < 3:
                    p.on("vector", fn)
                else:
                    trev[pi] = p.inc("vector", fn, "VV", 1)
    es1.close()
    p.bar()

    es2 = ExitStack()
    wsl = es2.enter_context(nc.sbuf_tensor("wsl", [128, KC * 512], F32))
    brep = es2.enter_context(nc.sbuf_tensor("brep", [128, D], F32))
    pj_ps = [es2.enter_context(nc.psum_tensor(f"pjps{i}", [128, 512], F32)) for i in range(2)]
    ev1 = [es2.enter_context(nc.sbuf_tensor(f"ev1{i}", [128, 512], F32)) for i in range(2)]
    ev2 = [es2.enter_context(nc.sbuf_tensor(f"ev2{i}", [128, 512], F32)) for i in range(2)]
    ps_ev = [0, 0]
    ev1_free = [0, 0]
    ev2_free = [0, 0]
    p.wait("scalar", "SD", p0sd)

    def proj_tok_major(wt_dram, scr_dram, is_k, bias_rep_dram):
        p.wait("sync", "VV", p.cnt["VV"])
        ldb = p.inc("sync", lambda e: e.dma_start(out=brep.ap(), in_=bias_rep_dram.ap()), "SD", 16)
        for ds in range(4):
            p.wait("sync", "TT", p.cnt["TT"])
            ldw = p.inc("sync", lambda e, ds=ds: e.dma_start(
                out=wsl.ap(), in_=band_ap(wt_dram, 0, 128, KC, ds * 512, 512, D)), "SD", 16)
            for tc in range(NCH):
                pi = tc % 2
                p.wait("tensor", "SD", ldw)
                p.wait("tensor", "VV", ps_ev[pi])
                for kc in range(KC):
                    fn = lambda e, kc=kc, tc=tc, pi=pi: e.matmul(
                        pj_ps[pi][:, :],
                        A1[:, kc * TOK + tc * 128: kc * TOK + (tc + 1) * 128],
                        wsl[:, kc * 512:(kc + 1) * 512],
                        start=(kc == 0), stop=(kc == KC - 1))
                    if kc < KC - 1:
                        p.on("tensor", fn)
                    else:
                        mm = p.inc("tensor", fn, "TT", 1)
                p.wait("vector", "TT", mm)
                p.wait("vector", "SD", ldb)
                p.wait("vector", "SS" if is_k else "GD", ev1_free[pi])
                vv = p.inc("vector", lambda e, ds=ds, pi=pi: e.tensor_add(
                    ev1[pi][:, :], pj_ps[pi][:, :], brep[:, ds * 512:(ds + 1) * 512]), "VV", 1)
                ps_ev[pi] = vv
                if is_k:
                    p.wait("scalar", "VV", vv)
                    p.wait("scalar", "GD", ev2_free[pi])
                    ss = p.inc("scalar", lambda e, tc=tc, pi=pi: e.activation(
                        ev2[pi][:, :], ev1[pi][:, :], AF.Relu,
                        scale=dcol_sb[:, tc:tc + 1]), "SS", 1)
                    ev1_free[pi] = ss
                    src = ev2
                    p.wait("gpsimd", "SS", ss)
                else:
                    src = ev1
                    p.wait("gpsimd", "VV", vv)
                st = p.inc("gpsimd", lambda e, src=src, pi=pi, tc=tc, ds=ds: e.dma_start(
                    out=bass.AP(scr_dram, tc * D + ds * 512, [[NCH * D, 128], [1, 512]]),
                    in_=src[pi][:, :]), "GD", 16)
                if is_k:
                    ev2_free[pi] = st
                else:
                    ev1_free[pi] = st

    proj_tok_major(wkt, k_scr, True, bk_rep)
    proj_tok_major(wvt, v_scr, False, bv_rep)
    p.wait("gpsimd", "GD", p.cnt["GD"])
    es2.close()
    p.bar()

    es_kv = ExitStack()
    kv2_sb = es_kv.enter_context(nc.sbuf_tensor("kv2_sb", [128, D], F32))
    ks2_sb = es_kv.enter_context(nc.sbuf_tensor("ks2_sb", [128, 16], F32))
    es3 = ExitStack()
    kch = [es3.enter_context(nc.sbuf_tensor(f"kch{i}", [128, D], F32)) for i in range(2)]
    vch = [es3.enter_context(nc.sbuf_tensor(f"vch{i}", [128, D], F32)) for i in range(2)]
    kv_sb = es3.enter_context(nc.sbuf_tensor("kv_sb", [128, D], F32))
    ks_sb = es3.enter_context(nc.sbuf_tensor("ks_sb", [128, 16], F32))
    kv_ps = [es3.enter_context(nc.psum_tensor(f"kvps{i}", [128, 512], F32)) for i in range(4)]
    ksm_ps = es3.enter_context(nc.psum_tensor("ksmps", [128, 16], F32))
    kv_ev = [0, 0, 0, 0]
    ksm_ev = 0
    kv_ld = [0, 0]

    for tc in range(NCH):
        bi = tc % 2
        p.wait("gpsimd", "TT", kv_ld[bi])
        p.inc("gpsimd", lambda e, tc=tc, bi=bi: e.dma_start(
            out=kch[bi].ap(), in_=bass.AP(k_scr, tc * D, [[NCH * D, 128], [1, D]])), "GD", 16)
        lv = p.inc("gpsimd", lambda e, tc=tc, bi=bi: e.dma_start(
            out=vch[bi].ap(), in_=bass.AP(v_scr, tc * D, [[NCH * D, 128], [1, D]])), "GD", 16)
        p.wait("tensor", "GD", lv)
        p.wait("tensor", "VV", ksm_ev)
        for hgrp in range(4):
            p.wait("tensor", "VV", kv_ev[hgrp])
            for hh in range(4):
                h = hgrp * 4 + hh
                fn = lambda e, h=h, hh=hh, hgrp=hgrp, bi=bi: e.matmul(
                    kv_ps[hgrp][:, hh * 128:(hh + 1) * 128],
                    kch[bi][:, h * 128:(h + 1) * 128],
                    vch[bi][:, h * 128:(h + 1) * 128],
                    start=(hh == 0), stop=(hh == 3), skip_group_check=True)
                if hh < 3:
                    p.on("tensor", fn)
                else:
                    mm = p.inc("tensor", fn, "TT", 1)
            p.wait("vector", "TT", mm)
            if tc == 0:
                kv_ev[hgrp] = p.inc("vector", lambda e, hgrp=hgrp: e.tensor_copy(
                    kv_sb[:, hgrp * 512:(hgrp + 1) * 512], kv_ps[hgrp][:, :]), "VV", 1)
            else:
                kv_ev[hgrp] = p.inc("vector", lambda e, hgrp=hgrp: e.tensor_add(
                    kv_sb[:, hgrp * 512:(hgrp + 1) * 512],
                    kv_sb[:, hgrp * 512:(hgrp + 1) * 512], kv_ps[hgrp][:, :]), "VV", 1)
        for h in range(HD):
            fn = lambda e, h=h, bi=bi: e.matmul(
                ksm_ps[:, h:h + 1], kch[bi][:, h * 128:(h + 1) * 128], onesc.ap(),
                start=(h == 0), stop=(h == HD - 1), skip_group_check=True)
            if h < HD - 1:
                p.on("tensor", fn)
            else:
                mmk = p.inc("tensor", fn, "TT", 1)
        kv_ld[bi] = mmk
        p.wait("vector", "TT", mmk)
        if tc == 0:
            ksm_ev = p.inc("vector", lambda e: e.tensor_copy(ks_sb[:, :], ksm_ps[:, :]), "VV", 1)
        else:
            ksm_ev = p.inc("vector", lambda e: e.tensor_add(ks_sb[:, :], ks_sb[:, :], ksm_ps[:, :]), "VV", 1)
    p.wait("gpsimd", "VV", max(max(kv_ev), ksm_ev))
    p.inc("gpsimd", lambda e: e.dma_start(
        out=bass.AP(cc_in, 0, [[2064, 128], [1, D]]), in_=kv_sb.ap()), "GD", 16)
    ccs = p.inc("gpsimd", lambda e: e.dma_start(
        out=bass.AP(cc_in, D, [[2064, 128], [1, 16]]), in_=ks_sb.ap()), "GD", 16)
    es3.close()
    p.bar()

    p.wait("gpsimd", "GD", ccs)
    ccd = p.inc("gpsimd", lambda e: e.collective_compute(
        "AllReduce", ALU.add,
        replica_groups=[[0, 1], [2, 3], [4, 5], [6, 7]],
        ins=[cc_in.ap().opt()], outs=[cc_out.ap().opt()]), "CC", 1)
    p.wait("gpsimd", "CC", ccd)
    p.inc("gpsimd", lambda e: e.dma_start(
        out=kv2_sb.ap(), in_=bass.AP(cc_out, 0, [[2064, 128], [1, D]])), "GD", 16)
    kvr = p.inc("gpsimd", lambda e: e.dma_start(
        out=ks2_sb.ap(), in_=bass.AP(cc_out, D, [[2064, 128], [1, 16]])), "GD", 16)

    es4 = ExitStack()
    wsl4 = es4.enter_context(nc.sbuf_tensor("wsl4", [128, KC * 512], F32))
    drep_sb = es4.enter_context(nc.sbuf_tensor("drep_sb", [128, TOK], F32))
    q_ps = [es4.enter_context(nc.psum_tensor(f"qps{i}", [128, 512], F32)) for i in range(2)]
    tq4 = [es4.enter_context(nc.sbuf_tensor(f"tq4{i}", [128, 512], F32)) for i in range(2)]
    ldd = p.inc("sync", lambda e: e.dma_start(out=drep_sb.ap(), in_=decay_rep.ap()), "SD", 16)
    p.wait("scalar", "SD", p0sd)
    p.wait("vector", "SD", ldd)
    qev = [0, 0]
    tq4_free = [0, 0]
    for ds in range(4):
        p.wait("sync", "TT", p.cnt["TT"])
        ldw = p.inc("sync", lambda e, ds=ds: e.dma_start(
            out=wsl4.ap(), in_=band_ap(wqt, 0, 128, KC, ds * 512, 512, D)), "SD", 16)
        for db in range(4):
            for th in range(2):
                pi = (db * 2 + th) % 2
                p.wait("tensor", "SD", ldw)
                p.wait("tensor", "SS", qev[pi])
                for kc in range(KC):
                    fn = lambda e, kc=kc, db=db, th=th, pi=pi: e.matmul(
                        q_ps[pi][:, :],
                        wsl4[:, kc * 512 + db * 128: kc * 512 + (db + 1) * 128],
                        A1[:, kc * TOK + th * 512: kc * TOK + (th + 1) * 512],
                        start=(kc == 0), stop=(kc == KC - 1))
                    if kc < KC - 1:
                        p.on("tensor", fn)
                    else:
                        mm = p.inc("tensor", fn, "TT", 1)
                p.wait("scalar", "TT", mm)
                p.wait("scalar", "VV", tq4_free[pi])
                qev[pi] = ss = p.inc("scalar", lambda e, ds=ds, db=db, pi=pi: e.activation(
                    tq4[pi][:, :], q_ps[pi][:, :], AF.Relu,
                    bias=bq_sb[:, ds * 4 + db: ds * 4 + db + 1]), "SS", 1)
                p.wait("vector", "SS", ss)
                tq4_free[pi] = p.inc("vector", lambda e, ds=ds, db=db, th=th, pi=pi: e.tensor_mul(
                    A2[:, (ds * 4 + db) * TOK + th * 512: (ds * 4 + db) * TOK + (th + 1) * 512],
                    tq4[pi][:, :], drep_sb[:, th * 512:(th + 1) * 512]), "VV", 1)
    es4.close()
    p.bar()

    es5 = ExitStack()
    out_ps = [es5.enter_context(nc.psum_tensor(f"ops{i}", [128, 512], F32)) for i in range(2)]
    dn_ps = es5.enter_context(nc.psum_tensor("dnps", [128, 512], F32))
    zb_ps = es5.enter_context(nc.psum_tensor("zbps", [128, 512], F32))
    zrow = es5.enter_context(nc.sbuf_tensor("zrow", [1, 512], F32))
    zrow2 = es5.enter_context(nc.sbuf_tensor("zrow2", [1, 512], F32))
    zb_sb = [es5.enter_context(nc.sbuf_tensor(f"zbsb{i}", [128, 512], F32)) for i in range(2)]
    o_ev = [0, 0]
    dn_ev_s = 0
    z_used = 0
    z_used_v = 0
    zb_free = [0, 0]
    p.wait("tensor", "GD", kvr)
    for h in range(HD):
        for th in range(2):
            pi = th
            p.wait("tensor", "VV", o_ev[pi])
            p.inc("tensor", lambda e, h=h, th=th, pi=pi: e.matmul(
                out_ps[pi][:, :], kv2_sb[:, h * 128:(h + 1) * 128],
                A2[:, h * TOK + th * 512: h * TOK + (th + 1) * 512],
                start=True, stop=True), "TT", 1)
            p.wait("tensor", "SS", dn_ev_s)
            md = p.inc("tensor", lambda e, h=h, th=th: e.matmul(
                dn_ps[0:1, :], ks2_sb[:, h:h + 1],
                A2[:, h * TOK + th * 512: h * TOK + (th + 1) * 512],
                start=True, stop=True), "TT", 1)
            p.wait("scalar", "TT", md)
            p.wait("scalar", "VV", z_used_v)
            sz = p.inc("scalar", lambda e: e.activation(
                zrow[0:1, :], dn_ps[0:1, :], AF.Identity, bias=cst[0:1, 4:5]), "SS", 1)
            dn_ev_s = sz
            p.wait("vector", "SS", sz)
            p.wait("vector", "TT", z_used)
            p.on("vector", lambda e: e.reciprocal(zrow2[0:1, :], zrow[0:1, :]))
            zv = p.inc("vector", lambda e: e.memset(stat[:, 91:92], 0.0), "VV", 1)
            p.wait("tensor", "VV", zv)
            p.wait("tensor", "VV", zb_free[pi])
            z_used = mz = p.inc("tensor", lambda e: e.matmul(
                zb_ps[:, :], ones1.ap(), zrow2[0:1, :], start=True, stop=True), "TT", 1)
            z_used_v = zv
            p.wait("vector", "TT", mz)
            p.inc("vector", lambda e, pi=pi: e.tensor_copy(zb_sb[pi][:, :], zb_ps[:, :]), "VV", 1)
            o_ev[pi] = zb_free[pi] = p.inc("vector", lambda e, h=h, th=th, pi=pi: e.tensor_mul(
                A1[:, h * TOK + th * 512: h * TOK + (th + 1) * 512],
                out_ps[pi][:, :], zb_sb[pi][:, :]), "VV", 1)
    es5.close()
    es_kv.close()
    p.bar()

    es6 = ExitStack()
    wsl6 = es6.enter_context(nc.sbuf_tensor("wsl6", [128, KC * 512], F32))
    brep6 = es6.enter_context(nc.sbuf_tensor("brep6", [128, D], F32))
    xsl = [es6.enter_context(nc.sbuf_tensor(f"xsl{i}", [128, 512], F32)) for i in range(2)]
    o_psb = [es6.enter_context(nc.psum_tensor(f"opsb{i}", [128, 512], F32)) for i in range(2)]
    tt6 = [es6.enter_context(nc.sbuf_tensor(f"tt6{i}", [128, 512], F32)) for i in range(2)]
    ldb6 = p.inc("sync", lambda e: e.dma_start(out=brep6.ap(), in_=bo_rep.ap()), "SD", 16)
    oev6 = [0, 0]
    tt6_free = [0, 0]
    xsl_free = [0, 0]
    for ds in range(4):
        p.wait("sync", "TT", p.cnt["TT"])
        ldw = p.inc("sync", lambda e, ds=ds: e.dma_start(
            out=wsl6.ap(), in_=band_ap(wot, 0, 128, KC, ds * 512, 512, D)), "SD", 16)
        for tc in range(NCH):
            pi = tc % 2
            p.wait("sync", "VV", xsl_free[pi])
            lx = p.inc("sync", lambda e, tc=tc, ds=ds, pi=pi: e.dma_start(
                out=xsl[pi].ap(),
                in_=bass.AP(x_own, tc * 128 * D + ds * 512, [[D, 128], [1, 512]])), "SD", 16)
            p.wait("tensor", "SD", ldw)
            p.wait("tensor", "VV", oev6[pi])
            for ec in range(KC):
                fn = lambda e, ec=ec, tc=tc, pi=pi: e.matmul(
                    o_psb[pi][:, :],
                    A1[:, ec * TOK + tc * 128: ec * TOK + (tc + 1) * 128],
                    wsl6[:, ec * 512:(ec + 1) * 512],
                    start=(ec == 0), stop=(ec == KC - 1))
                if ec < KC - 1:
                    p.on("tensor", fn)
                else:
                    mm = p.inc("tensor", fn, "TT", 1)
            p.wait("vector", "TT", mm)
            p.wait("vector", "SD", ldb6)
            p.wait("vector", "VV", tt6_free[pi])
            oev6[pi] = p.inc("vector", lambda e, ds=ds, pi=pi: e.tensor_add(
                tt6[pi][:, :], o_psb[pi][:, :], brep6[:, ds * 512:(ds + 1) * 512]), "VV", 1)
            p.wait("vector", "SD", lx)
            v2 = p.inc("vector", lambda e, tc=tc, ds=ds, pi=pi: e.tensor_add(
                A2[:, tc * D + ds * 512: tc * D + (ds + 1) * 512], tt6[pi][:, :], xsl[pi][:, :]), "VV", 1)
            tt6_free[pi] = v2
            xsl_free[pi] = v2
    es6.close()
    p.bar()

    es6b = ExitStack()
    ln_dump2 = es6b.enter_context(nc.sbuf_tensor("ln_dump2", [128, D], F32))
    xln2 = es6b.enter_context(nc.sbuf_tensor("xln2", [128, D], F32))
    tr_ps2 = [es6b.enter_context(nc.psum_tensor(f"trps2{i}", [128, 512], F32)) for i in range(2)]
    xln2_free = 0
    trev = [0, 0]
    x2_spill = 0
    for tc in range(NCH):
        p.wait("scalar", "TT", xln2_free)
        sdone = emit_ln(lambda tc=tc: A2[:, tc * D:(tc + 1) * D], lambda: xln2[:, :], (tc % 8) * 10, ln_dump2)
        x2_spill = p.inc("gpsimd", lambda e, tc=tc: e.dma_start(
            out=bass.AP(x2_scr, tc * D, [[NCH * D, 128], [1, D]]),
            in_=A2[:, tc * D:(tc + 1) * D]), "GD", 16)
        for g in range(4):
            pi = g % 2
            p.wait("tensor", "SS", sdone)
            p.wait("tensor", "VV", trev[pi])
            for j in range(4):
                band = g * 4 + j
                s8, par = band // 2, band % 2
                fn = lambda e, s8=s8, par=par, j=j, pi=pi: e.transpose(
                    tr_ps2[pi][:, j * 128:(j + 1) * 128],
                    xln2[:, 256 * s8 + par: 256 * s8 + 256: 2], ident.ap())
                if j < 3:
                    p.on("tensor", fn)
                else:
                    tdone = p.inc("tensor", fn, "TT", 1)
            if g == 3:
                xln2_free = tdone
            p.wait("vector", "TT", tdone)
            for j in range(4):
                fn = lambda e, g=g, tc=tc, pi=pi, j=j: e.tensor_copy(
                    A1[:, (g * 4 + j) * TOK + tc * 128: (g * 4 + j) * TOK + (tc + 1) * 128],
                    tr_ps2[pi][:, j * 128:(j + 1) * 128])
                if j < 3:
                    p.on("vector", fn)
                else:
                    trev[pi] = p.inc("vector", fn, "VV", 1)
    p.wait("gpsimd", "GD", p.cnt["GD"])
    es6b.close()
    p.bar()

    def emit_decode(dec, bi, stage_ap, w, scale_lo_ap, scale_hi_ap, out_lo_ap, out_hi_ap,
                    stage_gd, wdec_free_tt, scale_sd):
        ni, c_f, sgn, mp, ta, tb = (dec["ni"][bi], dec["c_f"][bi], dec["sgn"][bi],
                                    dec["mp"][bi], dec["ta"][bi], dec["tb"][bi])
        w2 = 2 * w
        p.wait("vector", "GD", stage_gd)
        p.wait("vector", "GG", dec["ni_free"][bi])
        p.on("vector", lambda e: e.tensor_single_scalar(ni[:, 0:w], stage_ap, 15, ALU.bitwise_and))
        p.on("vector", lambda e: e.tensor_single_scalar(
            ni[:, w:w2], stage_ap, 4, ALU.logical_shift_right))
        p.wait("vector", "SS", dec["cf_free"][bi])
        v1 = p.inc("vector", lambda e: e.tensor_copy(c_f[:, 0:w2], ni[:, 0:w2]), "VV", 1)
        p.wait("scalar", "VV", v1)
        s1 = p.inc("scalar", lambda e: e.activation(
            sgn[:, 0:w2], c_f[:, 0:w2], AF.Sign, bias=cst[:, 0:1], scale=-1.0), "SS", 1)
        dec["cf_free"][bi] = s1
        p.wait("vector", "SS", s1)
        v2 = p.inc("vector", lambda e: e.scalar_tensor_tensor(
            mp[:, 0:w2], sgn[:, 0:w2], 4.0, c_f[:, 0:w2], ALU.mult, ALU.add), "VV", 1)
        p.wait("scalar", "VV", v2)
        p.on("scalar", lambda e: e.activation(ta[:, 0:w2], mp[:, 0:w2], AF.Relu, bias=cst[:, 1:2]))
        s2 = p.inc("scalar", lambda e: e.activation(
            tb[:, 0:w2], mp[:, 0:w2], AF.Relu, bias=cst[:, 2:3]), "SS", 1)
        p.wait("vector", "SS", s2)
        p.on("vector", lambda e: e.scalar_tensor_tensor(
            ta[:, 0:w2], ta[:, 0:w2], 0.5, tb[:, 0:w2], ALU.mult, ALU.add))
        v3 = p.inc("vector", lambda e: e.scalar_tensor_tensor(
            mp[:, 0:w2], mp[:, 0:w2], 0.5, ta[:, 0:w2], ALU.mult, ALU.add), "VV", 1)
        p.wait("gpsimd", "VV", v3)
        g2 = p.inc("gpsimd", lambda e: e.tensor_mul(tb[:, 0:w2], mp[:, 0:w2], sgn[:, 0:w2]), "GG", 1)
        p.wait("vector", "GG", g2)
        v4 = p.inc("vector", lambda e: e.scalar_tensor_tensor(
            c_f[:, 0:w2], sgn[:, 0:w2], -2.0, tb[:, 0:w2], ALU.mult, ALU.add), "VV", 1)
        p.wait("gpsimd", "VV", v4)
        p.wait("gpsimd", "TT", wdec_free_tt)
        p.wait("gpsimd", "SD", scale_sd)
        p.on("gpsimd", lambda e: e.tensor_mul(out_lo_ap, c_f[:, 0:w], scale_lo_ap))
        g4 = p.inc("gpsimd", lambda e: e.tensor_mul(out_hi_ap, c_f[:, w:w2], scale_hi_ap), "GG", 1)
        dec["ni_free"][bi] = g4
        return g4

    def mk_dec(es, w, tag):
        d = {"ni": [es.enter_context(nc.sbuf_tensor(f"dni{tag}{i}", [128, 2 * w], I32)) for i in range(2)]}
        for nm in ("c_f", "sgn", "mp", "ta", "tb"):
            d[nm] = [es.enter_context(nc.sbuf_tensor(f"d{nm}{tag}{i}", [128, 2 * w], F32)) for i in range(2)]
        for nm in ("ni_free", "cf_free"):
            d[nm] = [0, 0]
        return d

    es7 = ExitStack()
    stg7 = [es7.enter_context(nc.sbuf_tensor(f"stg7{i}", [128, 2048], I32)) for i in range(2)]
    it7 = [es7.enter_context(nc.sbuf_tensor("it70", [128, 2048], F32))]
    dec7 = mk_dec(es7, 256, "a")
    scl7 = [es7.enter_context(nc.sbuf_tensor(f"scl7{i}", [128, 2048], F32)) for i in range(2)]
    gel7 = es7.enter_context(nc.sbuf_tensor("gel7", [128, 2048], F32))
    gu_ps = [es7.enter_context(nc.psum_tensor(f"gups{i}", [128, 512], F32)) for i in range(2)]
    WD4 = [0, 4096, 8192, 12288]
    wdec_mm_done = [0, 0, 0, 0]
    dec_done = [0, 0, 0, 0]
    scale_free = [0, 0]
    stg_free = [0, 0]
    gu_ev = [0, 0]
    gu_ev_sem = ["SS", "SS"]
    gel_ss = [0, 0, 0, 0]
    gel_free = 0
    it7_free = [0]
    ld_sc = [0, 0]
    ld_st = [0, 0]
    p.wait("sync", "GD", x2_spill)
    p.wait("vector", "GD", x2_spill)
    p.wait("gpsimd", "GD", x2_spill)

    def p7_loads(sup, mi):
        pkt, scr = ((g_pkt, g_scr), (u_pkt, u_scr))[mi]
        p.wait("sync", "GG", scale_free[mi])
        ld_sc[mi] = p.inc("sync", lambda e, scr=scr, sup=sup, mi=mi: e.dma_start(
            out=scl7[mi].ap(),
            in_=band_ap(scr, 0, 128, 8, sup * 256, 256, MLP)), "SD", 16)
        p.wait("gpsimd", "VV", stg_free[mi])
        ld_st[mi] = p.inc("gpsimd", lambda e, pkt=pkt, sup=sup, mi=mi: e.dma_start(
            out=stg7[mi].ap(), in_=band_ap(pkt, 0, 128, 8, sup * 256, 256, MLP)), "GD", 16)

    def p7_decode(sup, mi):
        w = mi * 2 + sup % 2
        for j in range(8):
            v4 = emit_decode(
                dec7, j % 2,
                stg7[mi][:, j * 256:(j + 1) * 256], 256,
                scl7[mi][:, j * 256:(j + 1) * 256],
                scl7[mi][:, j * 256:(j + 1) * 256],
                bass.AP(A2, WD4[w] + (2 * j) * 256, [[16384, 128], [1, 256]]),
                bass.AP(A2, WD4[w] + (2 * j + 1) * 256, [[16384, 128], [1, 256]]),
                ld_st[mi], wdec_mm_done[w] if j == 0 else 0, ld_sc[mi])
        stg_free[mi] = p.cnt["VV"]
        scale_free[mi] = v4
        dec_done[w] = v4

    def p7_mms(sup, mi):
        nonlocal_gel = [0]
        w = mi * 2 + sup % 2
        bi7 = 0
        p.wait("tensor", "GG", dec_done[w])
        for nt in range(2):
            for th in range(2):
                t4 = nt * 2 + th
                pi = t4 % 2
                p.wait("tensor", gu_ev_sem[pi], gu_ev[pi])
                for kb in range(KC):
                    fn = lambda e, kb=kb, nt=nt, th=th, pi=pi, w=w: e.matmul(
                        gu_ps[pi][:, :],
                        bass.AP(A2, WD4[w] + kb * 256 + nt, [[16384, 128], [2, 128]]),
                        A1[:, kb * TOK + th * 512: kb * TOK + (th + 1) * 512],
                        start=(kb == 0), stop=(kb == KC - 1))
                    if kb < KC - 1:
                        p.on("tensor", fn)
                    else:
                        mm = p.inc("tensor", fn, "TT", 1)
                if t4 == 3:
                    wdec_mm_done[w] = mm
                if mi == 0:
                    p.wait("scalar", "TT", mm)
                    if t4 == 0:
                        p.wait("scalar", "VV", gel_free)
                    gel_ss[t4] = p.inc("scalar", lambda e, nt=nt, th=th, pi=pi: e.activation(
                        gel7[:, nt * 1024 + th * 512: nt * 1024 + (th + 1) * 512],
                        gu_ps[pi][:, :], AF.Gelu_apprx_tanh), "SS", 1)
                    gu_ev[pi] = gel_ss[t4]
                    gu_ev_sem[pi] = "SS"
                else:
                    p.wait("vector", "TT", mm)
                    p.wait("vector", "SS", gel_ss[t4])
                    if t4 == 0:
                        p.wait("vector", "GD", it7_free[bi7])
                    vv = p.inc("vector", lambda e, nt=nt, th=th, pi=pi, bi7=bi7: e.tensor_mul(
                        it7[bi7][:, nt * 1024 + th * 512: nt * 1024 + (th + 1) * 512],
                        gu_ps[pi][:, :],
                        gel7[:, nt * 1024 + th * 512: nt * 1024 + (th + 1) * 512]), "VV", 1)
                    gu_ev[pi] = vv
                    gu_ev_sem[pi] = "VV"
                    nonlocal_gel[0] = vv
                    if t4 == 3:
                        p.wait("gpsimd", "VV", vv)
                        for th2 in range(2):
                            it7_free[bi7] = p.inc(
                                "gpsimd", lambda e, sup=sup, bi7=bi7, th2=th2: e.dma_start(
                                    out=bass.AP(it_scr, th2 * 65536 + 2 * sup * 512,
                                                [[2 * 128 * 512, 128], [512, 2], [1, 512]]),
                                    in_=bass.AP(it7[bi7], th2 * 512,
                                                [[2048, 128], [1024, 2], [1, 512]])),
                                "GD", 16)
        return nonlocal_gel[0]

    p7_loads(0, 0)
    p7_loads(0, 1)
    p7_decode(0, 0)
    p7_decode(0, 1)
    for sup in range(NSUP):
        if sup + 1 < NSUP:
            p7_loads(sup + 1, 0)
            p7_loads(sup + 1, 1)
        p7_mms(sup, 0)
        if sup + 1 < NSUP:
            p7_decode(sup + 1, 0)
        ev = p7_mms(sup, 1)
        gel_free = ev
        if sup + 1 < NSUP:
            p7_decode(sup + 1, 1)
    p.wait("gpsimd", "GD", p.cnt["GD"])
    es7.close()
    p.bar()

    es8 = ExitStack()
    stg8 = es8.enter_context(nc.sbuf_tensor("stg8", [128, 2048], I32))
    scl8 = es8.enter_context(nc.sbuf_tensor("scl8", [128, 2048], F32))
    it8 = [es8.enter_context(nc.sbuf_tensor(f"it8{i}", [128, 2048], F32)) for i in range(2)]
    dec8 = mk_dec(es8, 256, "b")
    dn_ps8 = [es8.enter_context(nc.psum_tensor(f"dps8{i}", [128, 512], F32)) for i in range(2)]
    WD8 = [0, 8192]
    wd8_last_mm = [0, 0]
    stg8_free = [0]
    scl8_free = [0]
    acc_ev = [0, 0]
    it8_free = [0, 0]
    dec8_done = [0, 0]

    def p8_load(ch, jt):
        p.wait("gpsimd", "VV", stg8_free[0])
        ldp = p.inc("gpsimd", lambda e, ch=ch, jt=jt: e.dma_start(
            out=stg8.ap(), in_=band_ap(d_pkt, 256 * ch + 128 * jt, 128, 1, 0, 2048, D)), "GD", 16)
        p.wait("sync", "GG", scl8_free[0])
        lds = p.inc("sync", lambda e, ch=ch, jt=jt: e.dma_start(
            out=scl8.ap(), in_=band_ap(d_scr, 256 * ch + 128 * jt, 128, 1, 0, 2048, D)), "SD", 16)
        return ldp, lds

    def p8_decode_pass(ch, jt, hs, ldp, lds):
        wb = ch % 2
        v4 = emit_decode(
            dec8, hs % 2,
            stg8[:, hs * 256:(hs + 1) * 256], 256,
            scl8[:, hs * 256:(hs + 1) * 256],
            scl8[:, hs * 256:(hs + 1) * 256],
            bass.AP(A1, WD8[wb] + (2 * jt) * 2048 + hs * 256, [[16384, 128], [1, 256]]),
            bass.AP(A1, WD8[wb] + (2 * jt + 1) * 2048 + hs * 256, [[16384, 128], [1, 256]]),
            ldp, wd8_last_mm[wb] if (jt == 0 and hs == 0) else 0, lds)
        return v4

    def p8_decode_full(ch):
        ldp, lds = p8_load(ch, 0)
        for hs in range(8):
            v4 = p8_decode_pass(ch, 0, hs, ldp, lds)
        stg8_free[0] = p.cnt["VV"]
        scl8_free[0] = v4
        ldp, lds = p8_load(ch, 1)
        for hs in range(8):
            v4 = p8_decode_pass(ch, 1, hs, ldp, lds)
        stg8_free[0] = p.cnt["VV"]
        scl8_free[0] = v4
        dec8_done[ch % 2] = v4

    p8_decode_full(0)
    for ch in range(DCH):
        wb = ch % 2
        do_next = ch + 1 < DCH
        if do_next:
            ldp, lds = p8_load(ch + 1, 0)
        nstate = {"jt": 0, "hs": 0, "ldp": ldp if do_next else 0, "lds": lds if do_next else 0}

        def step_decode():
            if not do_next or nstate["jt"] > 1:
                return
            v4 = p8_decode_pass(ch + 1, nstate["jt"], nstate["hs"], nstate["ldp"], nstate["lds"])
            nstate["hs"] += 1
            if nstate["hs"] == 8:
                nstate["hs"] = 0
                nstate["jt"] += 1
                stg8_free[0] = p.cnt["VV"]
                scl8_free[0] = v4
                if nstate["jt"] == 1:
                    nstate["ldp"], nstate["lds"] = p8_load(ch + 1, 1)
                else:
                    dec8_done[(ch + 1) % 2] = v4

        p.wait("tensor", "GG", dec8_done[wb])
        for th in range(2):
            bi8 = th
            p.wait("gpsimd", "TT", it8_free[bi8])
            ldi = p.inc("gpsimd", lambda e, ch=ch, th=th, bi8=bi8: e.dma_start(
                out=it8[bi8].ap(),
                in_=bass.AP(it_scr, th * 65536 + 4 * ch * 512, [[2 * 128 * 512, 128], [1, 2048]])),
                "GD", 16)
            for hb in range(16):
                pi = hb % 2
                p.wait("tensor", "GD", ldi)
                p.wait("tensor", "VV", acc_ev[pi])
                for g4 in range(4):
                    fn = lambda e, g4=g4, hb=hb, pi=pi, wb=wb, bi8=bi8: e.matmul(
                        dn_ps8[pi][:, :],
                        bass.AP(A1, WD8[wb] + g4 * 2048 + hb * 128, [[16384, 128], [1, 128]]),
                        it8[bi8][:, g4 * 512:(g4 + 1) * 512],
                        start=(g4 == 0), stop=(g4 == 3))
                    if g4 < 3:
                        p.on("tensor", fn)
                    else:
                        mm = p.inc("tensor", fn, "TT", 1)
                if hb == 15:
                    it8_free[bi8] = mm
                    if th == 1:
                        wd8_last_mm[wb] = mm
                p.wait("vector", "TT", mm)
                if ch == 0:
                    acc_ev[pi] = p.inc("vector", lambda e, hb=hb, th=th, pi=pi: e.tensor_copy(
                        A2[:, hb * TOK + th * 512: hb * TOK + (th + 1) * 512], dn_ps8[pi][:, :]), "VV", 1)
                else:
                    acc_ev[pi] = p.inc("vector", lambda e, hb=hb, th=th, pi=pi: e.tensor_add(
                        A2[:, hb * TOK + th * 512: hb * TOK + (th + 1) * 512],
                        A2[:, hb * TOK + th * 512: hb * TOK + (th + 1) * 512], dn_ps8[pi][:, :]), "VV", 1)
                if hb % 2 == 1:
                    step_decode()
        while do_next and nstate["jt"] <= 1:
            step_decode()
    es8.close()
    p.bar()

    es9 = ExitStack()
    x2c = [es9.enter_context(nc.sbuf_tensor(f"x2c{i}", [128, D], F32)) for i in range(2)]
    y_sb = [es9.enter_context(nc.sbuf_tensor(f"ysb{i}", [128, D], F32)) for i in range(2)]
    tr9 = es9.enter_context(nc.psum_tensor("tr9", [128, 2048], F32))
    x2c_free = [0, 0]
    y_free = [0, 0]
    tr9_free = 0
    for tc in range(NCH):
        bi = tc % 2
        p.wait("gpsimd", "VV", x2c_free[bi])
        lx2 = p.inc("gpsimd", lambda e, tc=tc, bi=bi: e.dma_start(
            out=x2c[bi].ap(), in_=bass.AP(x2_scr, tc * D, [[NCH * D, 128], [1, D]])), "GD", 16)
        p.wait("tensor", "VV", tr9_free)
        for ht in range(16):
            fn = lambda e, ht=ht, tc=tc: e.transpose(
                tr9[:, ht * 128:(ht + 1) * 128],
                A2[:, ht * TOK + tc * 128: ht * TOK + (tc + 1) * 128], ident.ap())
            if ht < 15:
                p.on("tensor", fn)
            else:
                tdone = p.inc("tensor", fn, "TT", 1)
        p.wait("vector", "TT", tdone)
        p.wait("vector", "GD", lx2)
        p.wait("vector", "GD", y_free[bi])
        for q in range(4):
            fn = lambda e, bi=bi, q=q: e.tensor_add(
                y_sb[bi][:, q * 512:(q + 1) * 512], tr9[:, q * 512:(q + 1) * 512],
                x2c[bi][:, q * 512:(q + 1) * 512])
            if q < 3:
                p.on("vector", fn)
            else:
                tr9_free = vv = p.inc("vector", fn, "VV", 1)
        x2c_free[bi] = vv
        p.wait("gpsimd", "VV", vv)
        y_free[bi] = p.inc("gpsimd", lambda e, tc=tc, bi=bi: e.dma_start(
            out=y_out.ap()[tc * 128:(tc + 1) * 128, :], in_=y_sb[bi][:, :]), "GD", 16)
    p.wait("gpsimd", "GD", p.cnt["GD"])
    es9.close()

    @block_ctx.tensor
    def _(eng):
        p.replay("tensor", eng)

    @block_ctx.vector
    def _(eng):
        p.replay("vector", eng)

    @block_ctx.scalar
    def _(eng):
        p.replay("scalar", eng)

    @block_ctx.gpsimd
    def _(eng):
        p.replay("gpsimd", eng)

    @block_ctx.sync
    def _(eng):
        p.replay("sync", eng)

    es_root.close()
    return nc


def kernel(**inputs):
    x = np.asarray(inputs["L0_attn_x"], dtype=np.float32)
    assert x.shape == (4, SEQ, D)

    def bytes_of(a):
        a = np.asarray(a)
        return np.ascontiguousarray(a.view(np.uint8).reshape(a.shape[0], a.shape[1], 4)[:, :, 0])

    wq_t = np.ascontiguousarray(np.asarray(inputs["L0_attn_Wq"], dtype=np.float32).T)
    wk_t = np.ascontiguousarray(np.asarray(inputs["L0_attn_Wk"], dtype=np.float32).T)
    wv_t = np.ascontiguousarray(np.asarray(inputs["L0_attn_Wv"], dtype=np.float32).T)
    wo_t = np.ascontiguousarray(np.asarray(inputs["L0_attn_Wo"], dtype=np.float32).T)
    bq = np.ascontiguousarray(np.asarray(inputs["L0_attn_bq"], dtype=np.float32).reshape(D, 1))
    bk = np.ascontiguousarray(np.broadcast_to(np.asarray(inputs["L0_attn_bk"], dtype=np.float32), (128, D)))
    bv = np.ascontiguousarray(np.broadcast_to(np.asarray(inputs["L0_attn_bv"], dtype=np.float32), (128, D)))
    bo = np.ascontiguousarray(np.broadcast_to(np.asarray(inputs["L0_attn_bo"], dtype=np.float32), (128, D)))
    g_pk = np.ascontiguousarray(bytes_of(inputs["L0_ffn_gate_W_packed"]).T)
    u_pk = np.ascontiguousarray(bytes_of(inputs["L0_ffn_up_W_packed"]).T)
    d_pk = np.ascontiguousarray(bytes_of(inputs["L0_ffn_down_W_packed"]).T)
    g_sc = np.ascontiguousarray(np.repeat(np.asarray(inputs["L0_ffn_gate_scales"], dtype=np.float32).T, 16, axis=0))
    u_sc = np.ascontiguousarray(np.repeat(np.asarray(inputs["L0_ffn_up_scales"], dtype=np.float32).T, 16, axis=0))
    d_sc = np.ascontiguousarray(np.repeat(np.asarray(inputs["L0_ffn_down_scales"], dtype=np.float32).T, 16, axis=0))
    ident = np.eye(128, dtype=np.float32)
    pos = np.arange(SEQ, dtype=np.float64)
    decay = np.exp(-pos / float(SEQ)).astype(np.float32)

    nc = build_nc()
    in_maps = []
    for c in range(8):
        b, hf = c // 2, c % 2
        dcol = np.ascontiguousarray(decay[hf * TOK:(hf + 1) * TOK].reshape(TOK, 1))
        drep = np.ascontiguousarray(np.broadcast_to(decay[hf * TOK:(hf + 1) * TOK], (128, TOK)))
        in_maps.append({
            "x_own": np.ascontiguousarray(x[b, hf * TOK:(hf + 1) * TOK, :]),
            "wqt": wq_t, "wkt": wk_t, "wvt": wv_t, "wot": wo_t,
            "bq_col": bq, "bk_rep": bk, "bv_rep": bv, "bo_rep": bo,
            "decay_col": dcol, "decay_rep": drep, "ident_in": ident,
            "g_pkt": g_pk, "u_pkt": u_pk, "d_pkt": d_pk,
            "g_scr": g_sc, "u_scr": u_sc, "d_scr": d_sc,
        })
    res = run_bass_kernel_spmd(nc, in_maps, core_ids=list(range(8)))
    y = np.empty((4, SEQ, D), dtype=np.float32)
    for c in range(8):
        b, hf = c // 2, c % 2
        y[b, hf * TOK:(hf + 1) * TOK, :] = res.results[c]["y"]
    return y


# version 14
# speedup vs baseline: 1.3034x; 1.0530x over previous
from contextlib import ExitStack

import numpy as np
import concourse.bass as bass
import concourse.mybir as mybir
from concourse.bass_utils import run_bass_kernel_spmd

F32 = mybir.dt.float32
I32 = mybir.dt.int32
U8 = mybir.dt.uint8
AF = mybir.ActivationFunctionType
ALU = mybir.AluOpType
AX = mybir.AxisListType

D = 2048
TOK = 1024
NCH = 8
KC = 16
HD = 16
SEQ = 2048
MLP = 16384
NSUP = 64
DCH = 32
ENGINES = ("tensor", "vector", "scalar", "gpsimd", "sync")


class Prog:

    def __init__(self, sems):
        self.sems = sems
        self.cnt = {k: 0 for k in sems}
        self.streams = {e: [] for e in ENGINES}

    def on(self, eng, f):
        self.streams[eng].append(("op", f))

    def inc(self, eng, f, sem, k=1):
        self.cnt[sem] += k
        v = self.cnt[sem]
        self.streams[eng].append(("inc", f, sem, k))
        return v

    def wait(self, eng, sem, v):
        if v <= 0:
            return
        assert v <= self.cnt[sem], f"wait {sem}>={v} but only {self.cnt[sem]} emitted"
        self.streams[eng].append(("wait", sem, v))

    def bar(self):
        for e in ENGINES:
            self.inc(e, lambda eng: eng.nop(), "BAR", 1)
        t = self.cnt["BAR"]
        for e in ENGINES:
            self.wait(e, "BAR", t)

    def replay(self, eng_name, eng):
        for item in self.streams[eng_name]:
            if item[0] == "op":
                item[1](eng)
            elif item[0] == "inc":
                item[1](eng).then_inc(self.sems[item[2]], item[3])
            else:
                eng.wait_ge(self.sems[item[1]], item[2])


def band_ap(t, row0, nrows_per_band, nbands, col0, ncols, row_stride):
    return bass.AP(
        t,
        row0 * row_stride + col0,
        [[row_stride, nrows_per_band], [row_stride * nrows_per_band, nbands], [1, ncols]],
    )


def build_nc():
    nc = bass.Bass("TRN2", target_bir_lowering=False)

    x_own = nc.dram_tensor("x_own", [TOK, D], F32, kind="ExternalInput")
    wqt = nc.dram_tensor("wqt", [D, D], F32, kind="ExternalInput")
    wkt = nc.dram_tensor("wkt", [D, D], F32, kind="ExternalInput")
    wvt = nc.dram_tensor("wvt", [D, D], F32, kind="ExternalInput")
    wot = nc.dram_tensor("wot", [D, D], F32, kind="ExternalInput")
    bq_col = nc.dram_tensor("bq_col", [D, 1], F32, kind="ExternalInput")
    bo_rep = nc.dram_tensor("bo_rep", [128, D], F32, kind="ExternalInput")
    bk_rep = nc.dram_tensor("bk_rep", [128, D], F32, kind="ExternalInput")
    bv_rep = nc.dram_tensor("bv_rep", [128, D], F32, kind="ExternalInput")
    decay_col = nc.dram_tensor("decay_col", [TOK, 1], F32, kind="ExternalInput")
    decay_rep = nc.dram_tensor("decay_rep", [128, TOK], F32, kind="ExternalInput")
    ident_in = nc.dram_tensor("ident_in", [128, 128], F32, kind="ExternalInput")
    g_pks = nc.dram_tensor("g_pks", [D // 2, 2048], U8, kind="ExternalInput")
    u_pks = nc.dram_tensor("u_pks", [D // 2, 2048], U8, kind="ExternalInput")
    d_pks = nc.dram_tensor("d_pks", [1024, D], U8, kind="ExternalInput")
    g_scs = nc.dram_tensor("g_scs", [D // 2, 2048], F32, kind="ExternalInput")
    u_scs = nc.dram_tensor("u_scs", [D // 2, 2048], F32, kind="ExternalInput")
    d_scs = nc.dram_tensor("d_scs", [1024, D], F32, kind="ExternalInput")
    y_out = nc.dram_tensor("y", [TOK, D], F32, kind="ExternalOutput")

    k_scr = nc.dram_tensor("k_scr", [128, NCH * D], F32, kind="Internal")
    v_scr = nc.dram_tensor("v_scr", [128, NCH * D], F32, kind="Internal")
    x2_scr = nc.dram_tensor("x2_scr", [128, NCH * D], F32, kind="Internal")
    it_scr = nc.dram_tensor("it_scr", [128, 2 * 128 * 512], F32, kind="Internal")
    cc_in = nc.dram_tensor("cc_in", [128, 2064], F32, kind="Internal")
    cc_out = nc.dram_tensor("cc_out", [128, 2064], F32, kind="Internal")
    w_sh_g = nc.dram_tensor("w_sh_g", [128, 32768], F32, kind="Internal")
    w_sh_u = nc.dram_tensor("w_sh_u", [128, 32768], F32, kind="Internal")
    w_sh_d = nc.dram_tensor("w_sh_d", [128, 32768], F32, kind="Internal")
    w_ag_g = nc.dram_tensor("w_ag_g", [1024, 32768], F32, kind="Internal")
    w_ag_u = nc.dram_tensor("w_ag_u", [1024, 32768], F32, kind="Internal")
    w_ag_d = nc.dram_tensor("w_ag_d", [1024, 32768], F32, kind="Internal")

    es_root = ExitStack()
    sems = {n: es_root.enter_context(nc.semaphore(n))
            for n in ("BAR", "SD", "GD", "TT", "VV", "SS", "GG", "CC")}
    p = Prog(sems)

    A1 = nc.alloc_sbuf_tensor("A1", [128, 16384], F32)
    A2 = nc.alloc_sbuf_tensor("A2", [128, 16384], F32)
    ident = nc.alloc_sbuf_tensor("ident", [128, 128], F32)
    cst = nc.alloc_sbuf_tensor("cst", [128, 8], F32)
    ones1 = nc.alloc_sbuf_tensor("ones1", [1, 128], F32)
    onesc = nc.alloc_sbuf_tensor("onesc", [128, 1], F32)
    bq_sb = nc.alloc_sbuf_tensor("bq_sb", [128, 16], F32)
    dcol_sb = nc.alloc_sbuf_tensor("dcol_sb", [128, 8], F32)
    stat = nc.alloc_sbuf_tensor("stat", [128, 96], F32)

    block_ctx = es_root.enter_context(nc.Block())

    p.inc("sync", lambda e: e.dma_start(out=ident.ap(), in_=ident_in.ap()), "SD", 16)
    p.inc("sync", lambda e: e.dma_start(
        out=bq_sb.ap(), in_=bass.AP(bq_col, 0, [[1, 128], [128, 16], [1, 1]]),
        allow_slow_non_contiguous=True), "SD", 16)
    p0sd = p.inc("sync", lambda e: e.dma_start(
        out=dcol_sb.ap(), in_=bass.AP(decay_col, 0, [[1, 128], [128, 8], [1, 1]]),
        allow_slow_non_contiguous=True), "SD", 16)
    p.on("vector", lambda e: e.memset(cst[:, 0:1], 7.5))
    p.on("vector", lambda e: e.memset(cst[:, 1:2], -8.0))
    p.on("vector", lambda e: e.memset(cst[:, 2:3], -10.0))
    p.on("vector", lambda e: e.memset(cst[:, 3:4], 1e-5))
    p.on("vector", lambda e: e.memset(cst[:, 4:5], 1e-6))
    p.on("vector", lambda e: e.memset(ones1.ap(), 1.0))
    p.inc("vector", lambda e: e.memset(onesc.ap(), 1.0), "VV", 1)
    p.bar()

    def emit_decode(dec, bi, stage_ap, w, scale_lo_ap, scale_hi_ap, out_lo_ap, out_hi_ap,
                    stage_gd, wdec_free_tt, scale_sd):
        ni, c_f, sgn, mp, ta, tb = (dec["ni"][bi], dec["c_f"][bi], dec["sgn"][bi],
                                    dec["mp"][bi], dec["ta"][bi], dec["tb"][bi])
        w2 = 2 * w
        p.wait("vector", "GD", stage_gd)
        p.wait("vector", "GG", dec["ni_free"][bi])
        p.on("vector", lambda e: e.tensor_single_scalar(ni[:, 0:w], stage_ap, 15, ALU.bitwise_and))
        p.on("vector", lambda e: e.tensor_single_scalar(
            ni[:, w:w2], stage_ap, 4, ALU.logical_shift_right))
        p.wait("vector", "SS", dec["cf_free"][bi])
        v1 = p.inc("vector", lambda e: e.tensor_copy(c_f[:, 0:w2], ni[:, 0:w2]), "VV", 1)
        p.wait("scalar", "VV", v1)
        s1 = p.inc("scalar", lambda e: e.activation(
            sgn[:, 0:w2], c_f[:, 0:w2], AF.Sign, bias=cst[:, 0:1], scale=-1.0), "SS", 1)
        dec["cf_free"][bi] = s1
        p.wait("vector", "SS", s1)
        v2 = p.inc("vector", lambda e: e.scalar_tensor_tensor(
            mp[:, 0:w2], sgn[:, 0:w2], 4.0, c_f[:, 0:w2], ALU.mult, ALU.add), "VV", 1)
        p.wait("scalar", "VV", v2)
        p.on("scalar", lambda e: e.activation(ta[:, 0:w2], mp[:, 0:w2], AF.Relu, bias=cst[:, 1:2]))
        s2 = p.inc("scalar", lambda e: e.activation(
            tb[:, 0:w2], mp[:, 0:w2], AF.Relu, bias=cst[:, 2:3]), "SS", 1)
        p.wait("vector", "SS", s2)
        p.on("vector", lambda e: e.scalar_tensor_tensor(
            ta[:, 0:w2], ta[:, 0:w2], 0.5, tb[:, 0:w2], ALU.mult, ALU.add))
        v3 = p.inc("vector", lambda e: e.scalar_tensor_tensor(
            mp[:, 0:w2], mp[:, 0:w2], 0.5, ta[:, 0:w2], ALU.mult, ALU.add), "VV", 1)
        p.wait("gpsimd", "VV", v3)
        g2 = p.inc("gpsimd", lambda e: e.tensor_mul(tb[:, 0:w2], mp[:, 0:w2], sgn[:, 0:w2]), "GG", 1)
        p.wait("vector", "GG", g2)
        v4 = p.inc("vector", lambda e: e.scalar_tensor_tensor(
            c_f[:, 0:w2], sgn[:, 0:w2], -2.0, tb[:, 0:w2], ALU.mult, ALU.add), "VV", 1)
        p.wait("gpsimd", "VV", v4)
        if wdec_free_tt[1] > 0:
            p.wait("gpsimd", wdec_free_tt[0], wdec_free_tt[1])
        p.wait("gpsimd", "SD", scale_sd)
        p.on("gpsimd", lambda e: e.tensor_mul(out_lo_ap, c_f[:, 0:w], scale_lo_ap))
        g4 = p.inc("gpsimd", lambda e: e.tensor_mul(out_hi_ap, c_f[:, w:w2], scale_hi_ap), "GG", 1)
        dec["ni_free"][bi] = g4
        return g4

    def mk_dec(es, w, tag):
        d = {"ni": [es.enter_context(nc.sbuf_tensor(f"dni{tag}{i}", [128, 2 * w], I32)) for i in range(2)]}
        for nm in ("c_f", "sgn", "mp", "ta", "tb"):
            d[nm] = [es.enter_context(nc.sbuf_tensor(f"d{nm}{tag}{i}", [128, 2 * w], F32)) for i in range(2)]
        for nm in ("ni_free", "cf_free"):
            d[nm] = [0, 0]
        return d

    es05 = ExitStack()
    stg05 = [es05.enter_context(nc.sbuf_tensor(f"stg05{i}", [128, 2048], I32)) for i in range(2)]
    scl05 = [es05.enter_context(nc.sbuf_tensor(f"scl05{i}", [128, 2048], F32)) for i in range(2)]
    dec05 = mk_dec(es05, 256, "c")
    wtmp = es05.enter_context(nc.sbuf_tensor("wtmp", [128, 4096], F32))
    wtmp_free = [0]

    stg05_free = [0, 0]
    scl05_free = [0, 0]

    def p05_unit(dst, slot_col, in_pk_ap, in_sc_ap, mi, out_lo_fn, out_hi_fn):
        p.wait("sync", "GG", scl05_free[mi])
        ldsc = p.inc("sync", lambda e, a=in_sc_ap: e.dma_start(out=scl05[mi].ap(), in_=a), "SD", 16)
        p.wait("gpsimd", "VV", stg05_free[mi])
        ldpk = p.inc("gpsimd", lambda e, a=in_pk_ap: e.dma_start(out=stg05[mi].ap(), in_=a), "GD", 16)
        for j in range(8):
            g4 = emit_decode(
                dec05, j % 2,
                stg05[mi][:, j * 256:(j + 1) * 256], 256,
                scl05[mi][:, j * 256:(j + 1) * 256],
                scl05[mi][:, j * 256:(j + 1) * 256],
                out_lo_fn(j), out_hi_fn(j),
                ldpk, ("GD", wtmp_free[0]) if j == 0 else ("GD", 0), ldsc)
        stg05_free[mi] = p.cnt["VV"]
        scl05_free[mi] = g4
        p.wait("gpsimd", "GG", g4)
        wtmp_free[0] = p.inc("gpsimd", lambda e, dst=dst, slot_col=slot_col: e.dma_start(
            out=bass.AP(dst, slot_col, [[32768, 128], [1, 4096]]), in_=wtmp.ap()), "GD", 16)

    for sup_l in range(8):
        for mi, (pk_t, sc_t) in enumerate(((g_pks, g_scs), (u_pks, u_scs))):
            p05_unit(
                w_sh_g if mi == 0 else w_sh_u,
                sup_l * 4096,
                band_ap(pk_t, 0, 128, 8, sup_l * 256, 256, 2048),
                band_ap(sc_t, 0, 128, 8, sup_l * 256, 256, 2048),
                mi,
                lambda j: wtmp[:, (2 * j) * 256:(2 * j) * 256 + 256],
                lambda j: wtmp[:, (2 * j + 1) * 256:(2 * j + 1) * 256 + 256])
    for ch_l in range(4):
        for jt in range(2):
            row0 = ch_l * 256 + jt * 128
            p05_unit(
                w_sh_d,
                ch_l * 8192 + jt * 4096,
                bass.AP(d_pks, row0 * D, [[D, 128], [1, 2048]]),
                bass.AP(d_scs, row0 * D, [[D, 128], [1, 2048]]),
                jt % 2,
                lambda j: wtmp[:, j * 256:(j + 1) * 256],
                lambda j: wtmp[:, 2048 + j * 256: 2048 + (j + 1) * 256])
    p.wait("gpsimd", "GD", p.cnt["GD"])
    p.bar()
    for w_sh_t, w_ag_t in ((w_sh_g, w_ag_g), (w_sh_u, w_ag_u), (w_sh_d, w_ag_d)):
        agdone = p.inc("gpsimd", lambda e, a=w_sh_t, b=w_ag_t: e.collective_compute(
            "AllGather", ALU.bypass,
            replica_groups=[[0, 1, 2, 3, 4, 5, 6, 7]],
            ins=[a.ap().opt()], outs=[b.ap().opt()]), "CC", 1)
    es05.close()

    def emit_ln(src_fn, dst_fn, c0, dump):
        p.on("vector", lambda e: e.reduce_sum(
            stat[:, c0:c0 + 1], src_fn(), axis=AX.X, negate=True))
        v1 = p.inc("vector", lambda e: e.memset(stat[:, 88:89], 0.0), "VV", 1)
        p.wait("scalar", "VV", v1)
        s2 = p.inc("scalar", lambda e: e.mul(stat[:, c0 + 1:c0 + 2], stat[:, c0:c0 + 1], 1.0 / D), "SS", 1)
        p.wait("vector", "SS", s2)
        v2 = p.inc("vector", lambda e: e.tensor_scalar_add(
            stat[:, c0 + 3:c0 + 4], stat[:, c0 + 1:c0 + 2], 0.0), "VV", 1)
        p.wait("scalar", "VV", v2)
        p.on("scalar", lambda e, dump=dump: e.activation(
            dump[:, :], src_fn(), AF.Square, bias=stat[:, c0 + 1:c0 + 2],
            accum_out=stat[:, c0 + 2:c0 + 3]))
        s3 = p.inc("scalar", lambda e: e.copy(stat[:, 89:90], stat[:, 88:89]), "SS", 1)
        p.wait("vector", "SS", s3)
        v3 = p.inc("vector", lambda e: e.tensor_scalar(
            stat[:, c0 + 4:c0 + 5], stat[:, c0 + 2:c0 + 3], 1.0 / D, 1e-5,
            ALU.mult, ALU.add), "VV", 1)
        p.wait("scalar", "VV", v3)
        s4 = p.inc("scalar", lambda e: e.activation(
            stat[:, c0 + 5:c0 + 6], stat[:, c0 + 4:c0 + 5], AF.Sqrt), "SS", 1)
        p.wait("vector", "SS", s4)
        p.on("vector", lambda e: e.reciprocal(
            stat[:, c0 + 6:c0 + 7], stat[:, c0 + 5:c0 + 6]))
        v4 = p.inc("vector", lambda e: e.memset(stat[:, 90:91], 0.0), "VV", 1)
        p.wait("scalar", "VV", v4)
        s5 = p.inc("scalar", lambda e: e.activation(
            stat[:, c0 + 8:c0 + 9], stat[:, c0 + 6:c0 + 7], AF.Square), "SS", 1)
        p.wait("vector", "SS", s5)
        p.on("vector", lambda e: e.scalar_tensor_tensor(
            stat[:, c0 + 9:c0 + 10], stat[:, c0 + 8:c0 + 9], -0.5, stat[:, c0 + 4:c0 + 5],
            ALU.mult, ALU.mult))
        v5 = p.inc("vector", lambda e: e.tensor_scalar_mul(
            stat[:, c0 + 7:c0 + 8], stat[:, c0 + 6:c0 + 7], 1.5), "VV", 1)
        p.wait("scalar", "VV", v5)
        s6 = p.inc("scalar", lambda e: e.activation(
            stat[:, c0 + 5:c0 + 6], stat[:, c0 + 9:c0 + 10], AF.Identity,
            bias=stat[:, c0 + 7:c0 + 8], scale=stat[:, c0 + 6:c0 + 7]), "SS", 1)
        p.wait("vector", "SS", s6)
        v6 = p.inc("vector", lambda e: e.tensor_mul(
            stat[:, c0 + 3:c0 + 4], stat[:, c0 + 5:c0 + 6], stat[:, c0 + 1:c0 + 2]), "VV", 1)
        p.wait("scalar", "VV", v6)
        s7 = p.inc("scalar", lambda e: e.activation(
            dst_fn(), src_fn(), AF.Identity, bias=stat[:, c0 + 3:c0 + 4],
            scale=stat[:, c0 + 5:c0 + 6]), "SS", 1)
        return s7

    es1 = ExitStack()
    xch = [es1.enter_context(nc.sbuf_tensor(f"xch{i}", [128, D], F32)) for i in range(2)]
    ln_dump1 = es1.enter_context(nc.sbuf_tensor("ln_dump1", [128, D], F32))
    xln1 = es1.enter_context(nc.sbuf_tensor("xln1", [128, D], F32))
    tr_ps1 = [es1.enter_context(nc.psum_tensor(f"trps{i}", [128, 512], F32)) for i in range(2)]
    xln_free = 0
    trev = [0, 0]
    xch_free = [0, 0]
    p.wait("tensor", "SD", p0sd)
    for tc in range(NCH):
        bi = tc % 2
        p.wait("sync", "SS", xch_free[bi])
        ldx = p.inc("sync", lambda e, tc=tc, bi=bi: e.dma_start(
            out=xch[bi].ap(), in_=x_own.ap()[tc * 128:(tc + 1) * 128, :]), "SD", 16)
        p.wait("vector", "SD", ldx)
        p.wait("scalar", "SD", ldx)
        p.wait("scalar", "TT", xln_free)
        sdone = emit_ln(lambda bi=bi: xch[bi][:, :], lambda: xln1[:, :], (tc % 8) * 10, ln_dump1)
        xch_free[bi] = sdone
        for g in range(4):
            pi = g % 2
            p.wait("tensor", "SS", sdone)
            p.wait("tensor", "VV", trev[pi])
            for j in range(4):
                kt = g * 4 + j
                fn = lambda e, kt=kt, j=j, pi=pi: e.transpose(
                    tr_ps1[pi][:, j * 128:(j + 1) * 128], xln1[:, kt * 128:(kt + 1) * 128], ident.ap())
                if j < 3:
                    p.on("tensor", fn)
                else:
                    tdone = p.inc("tensor", fn, "TT", 1)
            if g == 3:
                xln_free = tdone
            p.wait("vector", "TT", tdone)
            for j in range(4):
                fn = lambda e, g=g, tc=tc, pi=pi, j=j: e.tensor_copy(
                    A1[:, (g * 4 + j) * TOK + tc * 128: (g * 4 + j) * TOK + (tc + 1) * 128],
                    tr_ps1[pi][:, j * 128:(j + 1) * 128])
                if j < 3:
                    p.on("vector", fn)
                else:
                    trev[pi] = p.inc("vector", fn, "VV", 1)
    es1.close()
    p.bar()

    es2 = ExitStack()
    wsl = es2.enter_context(nc.sbuf_tensor("wsl", [128, KC * 512], F32))
    brep = es2.enter_context(nc.sbuf_tensor("brep", [128, D], F32))
    pj_ps = [es2.enter_context(nc.psum_tensor(f"pjps{i}", [128, 512], F32)) for i in range(2)]
    ev1 = [es2.enter_context(nc.sbuf_tensor(f"ev1{i}", [128, 512], F32)) for i in range(2)]
    ev2 = [es2.enter_context(nc.sbuf_tensor(f"ev2{i}", [128, 512], F32)) for i in range(2)]
    ps_ev = [0, 0]
    ev1_free = [0, 0]
    ev2_free = [0, 0]
    p.wait("scalar", "SD", p0sd)

    def proj_tok_major(wt_dram, scr_dram, is_k, bias_rep_dram):
        p.wait("sync", "VV", p.cnt["VV"])
        ldb = p.inc("sync", lambda e: e.dma_start(out=brep.ap(), in_=bias_rep_dram.ap()), "SD", 16)
        for ds in range(4):
            p.wait("sync", "TT", p.cnt["TT"])
            ldw = p.inc("sync", lambda e, ds=ds: e.dma_start(
                out=wsl.ap(), in_=band_ap(wt_dram, 0, 128, KC, ds * 512, 512, D)), "SD", 16)
            for tc in range(NCH):
                pi = tc % 2
                p.wait("tensor", "SD", ldw)
                p.wait("tensor", "VV", ps_ev[pi])
                for kc in range(KC):
                    fn = lambda e, kc=kc, tc=tc, pi=pi: e.matmul(
                        pj_ps[pi][:, :],
                        A1[:, kc * TOK + tc * 128: kc * TOK + (tc + 1) * 128],
                        wsl[:, kc * 512:(kc + 1) * 512],
                        start=(kc == 0), stop=(kc == KC - 1))
                    if kc < KC - 1:
                        p.on("tensor", fn)
                    else:
                        mm = p.inc("tensor", fn, "TT", 1)
                p.wait("vector", "TT", mm)
                p.wait("vector", "SD", ldb)
                p.wait("vector", "SS" if is_k else "GD", ev1_free[pi])
                vv = p.inc("vector", lambda e, ds=ds, pi=pi: e.tensor_add(
                    ev1[pi][:, :], pj_ps[pi][:, :], brep[:, ds * 512:(ds + 1) * 512]), "VV", 1)
                ps_ev[pi] = vv
                if is_k:
                    p.wait("scalar", "VV", vv)
                    p.wait("scalar", "GD", ev2_free[pi])
                    ss = p.inc("scalar", lambda e, tc=tc, pi=pi: e.activation(
                        ev2[pi][:, :], ev1[pi][:, :], AF.Relu,
                        scale=dcol_sb[:, tc:tc + 1]), "SS", 1)
                    ev1_free[pi] = ss
                    src = ev2
                    p.wait("gpsimd", "SS", ss)
                else:
                    src = ev1
                    p.wait("gpsimd", "VV", vv)
                st = p.inc("gpsimd", lambda e, src=src, pi=pi, tc=tc, ds=ds: e.dma_start(
                    out=bass.AP(scr_dram, tc * D + ds * 512, [[NCH * D, 128], [1, 512]]),
                    in_=src[pi][:, :]), "GD", 16)
                if is_k:
                    ev2_free[pi] = st
                else:
                    ev1_free[pi] = st

    proj_tok_major(wkt, k_scr, True, bk_rep)
    proj_tok_major(wvt, v_scr, False, bv_rep)
    p.wait("gpsimd", "GD", p.cnt["GD"])
    es2.close()
    p.bar()

    es_kv = ExitStack()
    kv2_sb = es_kv.enter_context(nc.sbuf_tensor("kv2_sb", [128, D], F32))
    ks2_sb = es_kv.enter_context(nc.sbuf_tensor("ks2_sb", [128, 16], F32))
    es3 = ExitStack()
    kch = [es3.enter_context(nc.sbuf_tensor(f"kch{i}", [128, D], F32)) for i in range(2)]
    vch = [es3.enter_context(nc.sbuf_tensor(f"vch{i}", [128, D], F32)) for i in range(2)]
    kv_sb = es3.enter_context(nc.sbuf_tensor("kv_sb", [128, D], F32))
    ks_sb = es3.enter_context(nc.sbuf_tensor("ks_sb", [128, 16], F32))
    kv_ps = [es3.enter_context(nc.psum_tensor(f"kvps{i}", [128, 512], F32)) for i in range(4)]
    ksm_ps = es3.enter_context(nc.psum_tensor("ksmps", [128, 16], F32))
    kv_ev = [0, 0, 0, 0]
    ksm_ev = 0
    kv_ld = [0, 0]

    for tc in range(NCH):
        bi = tc % 2
        p.wait("gpsimd", "TT", kv_ld[bi])
        p.inc("gpsimd", lambda e, tc=tc, bi=bi: e.dma_start(
            out=kch[bi].ap(), in_=bass.AP(k_scr, tc * D, [[NCH * D, 128], [1, D]])), "GD", 16)
        lv = p.inc("gpsimd", lambda e, tc=tc, bi=bi: e.dma_start(
            out=vch[bi].ap(), in_=bass.AP(v_scr, tc * D, [[NCH * D, 128], [1, D]])), "GD", 16)
        p.wait("tensor", "GD", lv)
        p.wait("tensor", "VV", ksm_ev)
        for hgrp in range(4):
            p.wait("tensor", "VV", kv_ev[hgrp])
            for hh in range(4):
                h = hgrp * 4 + hh
                fn = lambda e, h=h, hh=hh, hgrp=hgrp, bi=bi: e.matmul(
                    kv_ps[hgrp][:, hh * 128:(hh + 1) * 128],
                    kch[bi][:, h * 128:(h + 1) * 128],
                    vch[bi][:, h * 128:(h + 1) * 128],
                    start=(hh == 0), stop=(hh == 3), skip_group_check=True)
                if hh < 3:
                    p.on("tensor", fn)
                else:
                    mm = p.inc("tensor", fn, "TT", 1)
            p.wait("vector", "TT", mm)
            if tc == 0:
                kv_ev[hgrp] = p.inc("vector", lambda e, hgrp=hgrp: e.tensor_copy(
                    kv_sb[:, hgrp * 512:(hgrp + 1) * 512], kv_ps[hgrp][:, :]), "VV", 1)
            else:
                kv_ev[hgrp] = p.inc("vector", lambda e, hgrp=hgrp: e.tensor_add(
                    kv_sb[:, hgrp * 512:(hgrp + 1) * 512],
                    kv_sb[:, hgrp * 512:(hgrp + 1) * 512], kv_ps[hgrp][:, :]), "VV", 1)
        for h in range(HD):
            fn = lambda e, h=h, bi=bi: e.matmul(
                ksm_ps[:, h:h + 1], kch[bi][:, h * 128:(h + 1) * 128], onesc.ap(),
                start=(h == 0), stop=(h == HD - 1), skip_group_check=True)
            if h < HD - 1:
                p.on("tensor", fn)
            else:
                mmk = p.inc("tensor", fn, "TT", 1)
        kv_ld[bi] = mmk
        p.wait("vector", "TT", mmk)
        if tc == 0:
            ksm_ev = p.inc("vector", lambda e: e.tensor_copy(ks_sb[:, :], ksm_ps[:, :]), "VV", 1)
        else:
            ksm_ev = p.inc("vector", lambda e: e.tensor_add(ks_sb[:, :], ks_sb[:, :], ksm_ps[:, :]), "VV", 1)
    p.wait("gpsimd", "VV", max(max(kv_ev), ksm_ev))
    p.inc("gpsimd", lambda e: e.dma_start(
        out=bass.AP(cc_in, 0, [[2064, 128], [1, D]]), in_=kv_sb.ap()), "GD", 16)
    ccs = p.inc("gpsimd", lambda e: e.dma_start(
        out=bass.AP(cc_in, D, [[2064, 128], [1, 16]]), in_=ks_sb.ap()), "GD", 16)
    es3.close()
    p.bar()

    p.wait("gpsimd", "GD", ccs)
    ccd = p.inc("gpsimd", lambda e: e.collective_compute(
        "AllReduce", ALU.add,
        replica_groups=[[0, 1], [2, 3], [4, 5], [6, 7]],
        ins=[cc_in.ap().opt()], outs=[cc_out.ap().opt()]), "CC", 1)
    p.wait("gpsimd", "CC", ccd)
    p.inc("gpsimd", lambda e: e.dma_start(
        out=kv2_sb.ap(), in_=bass.AP(cc_out, 0, [[2064, 128], [1, D]])), "GD", 16)
    kvr = p.inc("gpsimd", lambda e: e.dma_start(
        out=ks2_sb.ap(), in_=bass.AP(cc_out, D, [[2064, 128], [1, 16]])), "GD", 16)

    es4 = ExitStack()
    wsl4 = es4.enter_context(nc.sbuf_tensor("wsl4", [128, KC * 512], F32))
    drep_sb = es4.enter_context(nc.sbuf_tensor("drep_sb", [128, TOK], F32))
    q_ps = [es4.enter_context(nc.psum_tensor(f"qps{i}", [128, 512], F32)) for i in range(2)]
    tq4 = [es4.enter_context(nc.sbuf_tensor(f"tq4{i}", [128, 512], F32)) for i in range(2)]
    ldd = p.inc("sync", lambda e: e.dma_start(out=drep_sb.ap(), in_=decay_rep.ap()), "SD", 16)
    p.wait("scalar", "SD", p0sd)
    p.wait("vector", "SD", ldd)
    qev = [0, 0]
    tq4_free = [0, 0]
    for ds in range(4):
        p.wait("sync", "TT", p.cnt["TT"])
        ldw = p.inc("sync", lambda e, ds=ds: e.dma_start(
            out=wsl4.ap(), in_=band_ap(wqt, 0, 128, KC, ds * 512, 512, D)), "SD", 16)
        for db in range(4):
            for th in range(2):
                pi = (db * 2 + th) % 2
                p.wait("tensor", "SD", ldw)
                p.wait("tensor", "SS", qev[pi])
                for kc in range(KC):
                    fn = lambda e, kc=kc, db=db, th=th, pi=pi: e.matmul(
                        q_ps[pi][:, :],
                        wsl4[:, kc * 512 + db * 128: kc * 512 + (db + 1) * 128],
                        A1[:, kc * TOK + th * 512: kc * TOK + (th + 1) * 512],
                        start=(kc == 0), stop=(kc == KC - 1))
                    if kc < KC - 1:
                        p.on("tensor", fn)
                    else:
                        mm = p.inc("tensor", fn, "TT", 1)
                p.wait("scalar", "TT", mm)
                p.wait("scalar", "VV", tq4_free[pi])
                qev[pi] = ss = p.inc("scalar", lambda e, ds=ds, db=db, pi=pi: e.activation(
                    tq4[pi][:, :], q_ps[pi][:, :], AF.Relu,
                    bias=bq_sb[:, ds * 4 + db: ds * 4 + db + 1]), "SS", 1)
                p.wait("vector", "SS", ss)
                tq4_free[pi] = p.inc("vector", lambda e, ds=ds, db=db, th=th, pi=pi: e.tensor_mul(
                    A2[:, (ds * 4 + db) * TOK + th * 512: (ds * 4 + db) * TOK + (th + 1) * 512],
                    tq4[pi][:, :], drep_sb[:, th * 512:(th + 1) * 512]), "VV", 1)
    es4.close()
    p.bar()

    es5 = ExitStack()
    out_ps = [es5.enter_context(nc.psum_tensor(f"ops{i}", [128, 512], F32)) for i in range(2)]
    dn_ps = es5.enter_context(nc.psum_tensor("dnps", [128, 512], F32))
    zb_ps = es5.enter_context(nc.psum_tensor("zbps", [128, 512], F32))
    zrow = es5.enter_context(nc.sbuf_tensor("zrow", [1, 512], F32))
    zrow2 = es5.enter_context(nc.sbuf_tensor("zrow2", [1, 512], F32))
    zb_sb = [es5.enter_context(nc.sbuf_tensor(f"zbsb{i}", [128, 512], F32)) for i in range(2)]
    o_ev = [0, 0]
    dn_ev_s = 0
    z_used = 0
    z_used_v = 0
    zb_free = [0, 0]
    p.wait("tensor", "GD", kvr)
    for h in range(HD):
        for th in range(2):
            pi = th
            p.wait("tensor", "VV", o_ev[pi])
            p.inc("tensor", lambda e, h=h, th=th, pi=pi: e.matmul(
                out_ps[pi][:, :], kv2_sb[:, h * 128:(h + 1) * 128],
                A2[:, h * TOK + th * 512: h * TOK + (th + 1) * 512],
                start=True, stop=True), "TT", 1)
            p.wait("tensor", "SS", dn_ev_s)
            md = p.inc("tensor", lambda e, h=h, th=th: e.matmul(
                dn_ps[0:1, :], ks2_sb[:, h:h + 1],
                A2[:, h * TOK + th * 512: h * TOK + (th + 1) * 512],
                start=True, stop=True), "TT", 1)
            p.wait("scalar", "TT", md)
            p.wait("scalar", "VV", z_used_v)
            sz = p.inc("scalar", lambda e: e.activation(
                zrow[0:1, :], dn_ps[0:1, :], AF.Identity, bias=cst[0:1, 4:5]), "SS", 1)
            dn_ev_s = sz
            p.wait("vector", "SS", sz)
            p.wait("vector", "TT", z_used)
            p.on("vector", lambda e: e.reciprocal(zrow2[0:1, :], zrow[0:1, :]))
            zv = p.inc("vector", lambda e: e.memset(stat[:, 91:92], 0.0), "VV", 1)
            p.wait("tensor", "VV", zv)
            p.wait("tensor", "VV", zb_free[pi])
            z_used = mz = p.inc("tensor", lambda e: e.matmul(
                zb_ps[:, :], ones1.ap(), zrow2[0:1, :], start=True, stop=True), "TT", 1)
            z_used_v = zv
            p.wait("vector", "TT", mz)
            p.inc("vector", lambda e, pi=pi: e.tensor_copy(zb_sb[pi][:, :], zb_ps[:, :]), "VV", 1)
            o_ev[pi] = zb_free[pi] = p.inc("vector", lambda e, h=h, th=th, pi=pi: e.tensor_mul(
                A1[:, h * TOK + th * 512: h * TOK + (th + 1) * 512],
                out_ps[pi][:, :], zb_sb[pi][:, :]), "VV", 1)
    es5.close()
    es_kv.close()
    p.bar()

    es6 = ExitStack()
    wsl6 = es6.enter_context(nc.sbuf_tensor("wsl6", [128, KC * 512], F32))
    brep6 = es6.enter_context(nc.sbuf_tensor("brep6", [128, D], F32))
    xsl = [es6.enter_context(nc.sbuf_tensor(f"xsl{i}", [128, 512], F32)) for i in range(2)]
    o_psb = [es6.enter_context(nc.psum_tensor(f"opsb{i}", [128, 512], F32)) for i in range(2)]
    tt6 = [es6.enter_context(nc.sbuf_tensor(f"tt6{i}", [128, 512], F32)) for i in range(2)]
    ldb6 = p.inc("sync", lambda e: e.dma_start(out=brep6.ap(), in_=bo_rep.ap()), "SD", 16)
    oev6 = [0, 0]
    tt6_free = [0, 0]
    xsl_free = [0, 0]
    for ds in range(4):
        p.wait("sync", "TT", p.cnt["TT"])
        ldw = p.inc("sync", lambda e, ds=ds: e.dma_start(
            out=wsl6.ap(), in_=band_ap(wot, 0, 128, KC, ds * 512, 512, D)), "SD", 16)
        for tc in range(NCH):
            pi = tc % 2
            p.wait("sync", "VV", xsl_free[pi])
            lx = p.inc("sync", lambda e, tc=tc, ds=ds, pi=pi: e.dma_start(
                out=xsl[pi].ap(),
                in_=bass.AP(x_own, tc * 128 * D + ds * 512, [[D, 128], [1, 512]])), "SD", 16)
            p.wait("tensor", "SD", ldw)
            p.wait("tensor", "VV", oev6[pi])
            for ec in range(KC):
                fn = lambda e, ec=ec, tc=tc, pi=pi: e.matmul(
                    o_psb[pi][:, :],
                    A1[:, ec * TOK + tc * 128: ec * TOK + (tc + 1) * 128],
                    wsl6[:, ec * 512:(ec + 1) * 512],
                    start=(ec == 0), stop=(ec == KC - 1))
                if ec < KC - 1:
                    p.on("tensor", fn)
                else:
                    mm = p.inc("tensor", fn, "TT", 1)
            p.wait("vector", "TT", mm)
            p.wait("vector", "SD", ldb6)
            p.wait("vector", "VV", tt6_free[pi])
            oev6[pi] = p.inc("vector", lambda e, ds=ds, pi=pi: e.tensor_add(
                tt6[pi][:, :], o_psb[pi][:, :], brep6[:, ds * 512:(ds + 1) * 512]), "VV", 1)
            p.wait("vector", "SD", lx)
            v2 = p.inc("vector", lambda e, tc=tc, ds=ds, pi=pi: e.tensor_add(
                A2[:, tc * D + ds * 512: tc * D + (ds + 1) * 512], tt6[pi][:, :], xsl[pi][:, :]), "VV", 1)
            tt6_free[pi] = v2
            xsl_free[pi] = v2
    es6.close()
    p.bar()

    es6b = ExitStack()
    ln_dump2 = es6b.enter_context(nc.sbuf_tensor("ln_dump2", [128, D], F32))
    xln2 = es6b.enter_context(nc.sbuf_tensor("xln2", [128, D], F32))
    tr_ps2 = [es6b.enter_context(nc.psum_tensor(f"trps2{i}", [128, 512], F32)) for i in range(2)]
    xln2_free = 0
    trev = [0, 0]
    x2_spill = 0
    for tc in range(NCH):
        p.wait("scalar", "TT", xln2_free)
        sdone = emit_ln(lambda tc=tc: A2[:, tc * D:(tc + 1) * D], lambda: xln2[:, :], (tc % 8) * 10, ln_dump2)
        x2_spill = p.inc("gpsimd", lambda e, tc=tc: e.dma_start(
            out=bass.AP(x2_scr, tc * D, [[NCH * D, 128], [1, D]]),
            in_=A2[:, tc * D:(tc + 1) * D]), "GD", 16)
        for g in range(4):
            pi = g % 2
            p.wait("tensor", "SS", sdone)
            p.wait("tensor", "VV", trev[pi])
            for j in range(4):
                band = g * 4 + j
                s8, par = band // 2, band % 2
                fn = lambda e, s8=s8, par=par, j=j, pi=pi: e.transpose(
                    tr_ps2[pi][:, j * 128:(j + 1) * 128],
                    xln2[:, 256 * s8 + par: 256 * s8 + 256: 2], ident.ap())
                if j < 3:
                    p.on("tensor", fn)
                else:
                    tdone = p.inc("tensor", fn, "TT", 1)
            if g == 3:
                xln2_free = tdone
            p.wait("vector", "TT", tdone)
            for j in range(4):
                fn = lambda e, g=g, tc=tc, pi=pi, j=j: e.tensor_copy(
                    A1[:, (g * 4 + j) * TOK + tc * 128: (g * 4 + j) * TOK + (tc + 1) * 128],
                    tr_ps2[pi][:, j * 128:(j + 1) * 128])
                if j < 3:
                    p.on("vector", fn)
                else:
                    trev[pi] = p.inc("vector", fn, "VV", 1)
    p.wait("gpsimd", "GD", p.cnt["GD"])
    es6b.close()
    p.bar()

    es7 = ExitStack()
    it7 = [es7.enter_context(nc.sbuf_tensor("it70", [128, 2048], F32))]
    gel7 = es7.enter_context(nc.sbuf_tensor("gel7", [128, 2048], F32))
    gu_ps = [es7.enter_context(nc.psum_tensor(f"gups{i}", [128, 512], F32)) for i in range(2)]
    WD4 = [0, 4096, 8192, 12288]
    wdec_mm_done = [0, 0, 0, 0]
    dec_done = [0, 0, 0, 0]
    gu_ev = [0, 0]
    gu_ev_sem = ["SS", "SS"]
    gel_ss = [0, 0, 0, 0]
    gel_free = 0
    it7_free = [0]
    p.wait("sync", "GD", x2_spill)
    p.wait("sync", "CC", agdone)

    def p7_load(sup, mi):
        w = mi * 2 + sup % 2
        rank, sl = sup // 8, sup % 8
        src = w_ag_g if mi == 0 else w_ag_u
        p.wait("sync", "TT", wdec_mm_done[w])
        dec_done[w] = p.inc("sync", lambda e, rank=rank, sl=sl, src=src, w=w: e.dma_start(
            out=bass.AP(A2, WD4[w], [[16384, 128], [1, 4096]]),
            in_=bass.AP(src, (128 * rank) * 32768 + sl * 4096,
                        [[32768, 128], [1, 4096]])), "SD", 16)

    def p7_mms(sup, mi):
        nonlocal_gel = [0]
        w = mi * 2 + sup % 2
        p.wait("tensor", "SD", dec_done[w])
        for nt in range(2):
            for th in range(2):
                t4 = nt * 2 + th
                pi = t4 % 2
                p.wait("tensor", gu_ev_sem[pi], gu_ev[pi])
                for kb in range(KC):
                    fn = lambda e, kb=kb, nt=nt, th=th, pi=pi, w=w: e.matmul(
                        gu_ps[pi][:, :],
                        bass.AP(A2, WD4[w] + kb * 256 + nt, [[16384, 128], [2, 128]]),
                        A1[:, kb * TOK + th * 512: kb * TOK + (th + 1) * 512],
                        start=(kb == 0), stop=(kb == KC - 1))
                    if kb < KC - 1:
                        p.on("tensor", fn)
                    else:
                        mm = p.inc("tensor", fn, "TT", 1)
                if t4 == 3:
                    wdec_mm_done[w] = mm
                if mi == 0:
                    p.wait("scalar", "TT", mm)
                    if t4 == 0:
                        p.wait("scalar", "VV", gel_free)
                    gel_ss[t4] = p.inc("scalar", lambda e, nt=nt, th=th, pi=pi: e.activation(
                        gel7[:, nt * 1024 + th * 512: nt * 1024 + (th + 1) * 512],
                        gu_ps[pi][:, :], AF.Gelu_apprx_tanh), "SS", 1)
                    gu_ev[pi] = gel_ss[t4]
                    gu_ev_sem[pi] = "SS"
                else:
                    p.wait("vector", "TT", mm)
                    p.wait("vector", "SS", gel_ss[t4])
                    if t4 == 0:
                        p.wait("vector", "GD", it7_free[0])
                    vv = p.inc("vector", lambda e, nt=nt, th=th, pi=pi: e.tensor_mul(
                        it7[0][:, nt * 1024 + th * 512: nt * 1024 + (th + 1) * 512],
                        gu_ps[pi][:, :],
                        gel7[:, nt * 1024 + th * 512: nt * 1024 + (th + 1) * 512]), "VV", 1)
                    gu_ev[pi] = vv
                    gu_ev_sem[pi] = "VV"
                    nonlocal_gel[0] = vv
                    if t4 == 3:
                        p.wait("gpsimd", "VV", vv)
                        for th2 in range(2):
                            it7_free[0] = p.inc(
                                "gpsimd", lambda e, sup=sup, th2=th2: e.dma_start(
                                    out=bass.AP(it_scr, th2 * 65536 + 2 * sup * 512,
                                                [[2 * 128 * 512, 128], [512, 2], [1, 512]]),
                                    in_=bass.AP(it7[0], th2 * 512,
                                                [[2048, 128], [1024, 2], [1, 512]])),
                                "GD", 16)
        return nonlocal_gel[0]

    p7_load(0, 0)
    p7_load(0, 1)
    for sup in range(NSUP):
        if sup + 1 < NSUP:
            p7_load(sup + 1, 0)
            p7_load(sup + 1, 1)
        p7_mms(sup, 0)
        gel_free = p7_mms(sup, 1)
    p.wait("gpsimd", "GD", p.cnt["GD"])
    es7.close()
    p.bar()

    es8 = ExitStack()
    it8 = [es8.enter_context(nc.sbuf_tensor(f"it8{i}", [128, 2048], F32)) for i in range(2)]
    dn_ps8 = [es8.enter_context(nc.psum_tensor(f"dps8{i}", [128, 512], F32)) for i in range(2)]
    WD8 = [0, 8192]
    wd8_last_mm = [0, 0]
    acc_ev = [0, 0]
    it8_free = [0, 0]
    dec8_done = [0, 0]

    def p8_load(ch):
        wb = ch % 2
        rank, cl = ch // 4, ch % 4
        p.wait("sync", "TT", wd8_last_mm[wb])
        dec8_done[wb] = p.inc("sync", lambda e, rank=rank, cl=cl, wb=wb: e.dma_start(
            out=bass.AP(A1, WD8[wb], [[16384, 128], [1, 8192]]),
            in_=bass.AP(w_ag_d, (128 * rank) * 32768 + cl * 8192,
                        [[32768, 128], [1, 8192]])), "SD", 16)

    p8_load(0)
    for ch in range(DCH):
        wb = ch % 2
        if ch + 1 < DCH:
            p8_load(ch + 1)
        p.wait("tensor", "SD", dec8_done[wb])
        for th in range(2):
            bi8 = th
            p.wait("gpsimd", "TT", it8_free[bi8])
            ldi = p.inc("gpsimd", lambda e, ch=ch, th=th, bi8=bi8: e.dma_start(
                out=it8[bi8].ap(),
                in_=bass.AP(it_scr, th * 65536 + 4 * ch * 512, [[2 * 128 * 512, 128], [1, 2048]])),
                "GD", 16)
            for hb in range(16):
                pi = hb % 2
                p.wait("tensor", "GD", ldi)
                p.wait("tensor", "VV", acc_ev[pi])
                for g4 in range(4):
                    fn = lambda e, g4=g4, hb=hb, pi=pi, wb=wb, bi8=bi8: e.matmul(
                        dn_ps8[pi][:, :],
                        bass.AP(A1, WD8[wb] + g4 * 2048 + hb * 128, [[16384, 128], [1, 128]]),
                        it8[bi8][:, g4 * 512:(g4 + 1) * 512],
                        start=(g4 == 0), stop=(g4 == 3))
                    if g4 < 3:
                        p.on("tensor", fn)
                    else:
                        mm = p.inc("tensor", fn, "TT", 1)
                if hb == 15:
                    it8_free[bi8] = mm
                    if th == 1:
                        wd8_last_mm[wb] = mm
                p.wait("vector", "TT", mm)
                if ch == 0:
                    acc_ev[pi] = p.inc("vector", lambda e, hb=hb, th=th, pi=pi: e.tensor_copy(
                        A2[:, hb * TOK + th * 512: hb * TOK + (th + 1) * 512], dn_ps8[pi][:, :]), "VV", 1)
                else:
                    acc_ev[pi] = p.inc("vector", lambda e, hb=hb, th=th, pi=pi: e.tensor_add(
                        A2[:, hb * TOK + th * 512: hb * TOK + (th + 1) * 512],
                        A2[:, hb * TOK + th * 512: hb * TOK + (th + 1) * 512], dn_ps8[pi][:, :]), "VV", 1)
    es8.close()
    p.bar()

    es9 = ExitStack()
    x2c = [es9.enter_context(nc.sbuf_tensor(f"x2c{i}", [128, D], F32)) for i in range(2)]
    y_sb = [es9.enter_context(nc.sbuf_tensor(f"ysb{i}", [128, D], F32)) for i in range(2)]
    tr9 = es9.enter_context(nc.psum_tensor("tr9", [128, 2048], F32))
    x2c_free = [0, 0]
    y_free = [0, 0]
    tr9_free = 0
    for tc in range(NCH):
        bi = tc % 2
        p.wait("gpsimd", "VV", x2c_free[bi])
        lx2 = p.inc("gpsimd", lambda e, tc=tc, bi=bi: e.dma_start(
            out=x2c[bi].ap(), in_=bass.AP(x2_scr, tc * D, [[NCH * D, 128], [1, D]])), "GD", 16)
        p.wait("tensor", "VV", tr9_free)
        for ht in range(16):
            fn = lambda e, ht=ht, tc=tc: e.transpose(
                tr9[:, ht * 128:(ht + 1) * 128],
                A2[:, ht * TOK + tc * 128: ht * TOK + (tc + 1) * 128], ident.ap())
            if ht < 15:
                p.on("tensor", fn)
            else:
                tdone = p.inc("tensor", fn, "TT", 1)
        p.wait("vector", "TT", tdone)
        p.wait("vector", "GD", lx2)
        p.wait("vector", "GD", y_free[bi])
        for q in range(4):
            fn = lambda e, bi=bi, q=q: e.tensor_add(
                y_sb[bi][:, q * 512:(q + 1) * 512], tr9[:, q * 512:(q + 1) * 512],
                x2c[bi][:, q * 512:(q + 1) * 512])
            if q < 3:
                p.on("vector", fn)
            else:
                tr9_free = vv = p.inc("vector", fn, "VV", 1)
        x2c_free[bi] = vv
        p.wait("gpsimd", "VV", vv)
        y_free[bi] = p.inc("gpsimd", lambda e, tc=tc, bi=bi: e.dma_start(
            out=y_out.ap()[tc * 128:(tc + 1) * 128, :], in_=y_sb[bi][:, :]), "GD", 16)
    p.wait("gpsimd", "GD", p.cnt["GD"])
    es9.close()

    @block_ctx.tensor
    def _(eng):
        p.replay("tensor", eng)

    @block_ctx.vector
    def _(eng):
        p.replay("vector", eng)

    @block_ctx.scalar
    def _(eng):
        p.replay("scalar", eng)

    @block_ctx.gpsimd
    def _(eng):
        p.replay("gpsimd", eng)

    @block_ctx.sync
    def _(eng):
        p.replay("sync", eng)

    es_root.close()
    return nc


def kernel(**inputs):
    x = np.asarray(inputs["L0_attn_x"], dtype=np.float32)
    assert x.shape == (4, SEQ, D)

    def bytes_of(a):
        a = np.asarray(a)
        return np.ascontiguousarray(a.view(np.uint8).reshape(a.shape[0], a.shape[1], 4)[:, :, 0])

    wq_t = np.ascontiguousarray(np.asarray(inputs["L0_attn_Wq"], dtype=np.float32).T)
    wk_t = np.ascontiguousarray(np.asarray(inputs["L0_attn_Wk"], dtype=np.float32).T)
    wv_t = np.ascontiguousarray(np.asarray(inputs["L0_attn_Wv"], dtype=np.float32).T)
    wo_t = np.ascontiguousarray(np.asarray(inputs["L0_attn_Wo"], dtype=np.float32).T)
    bq = np.ascontiguousarray(np.asarray(inputs["L0_attn_bq"], dtype=np.float32).reshape(D, 1))
    bk = np.ascontiguousarray(np.broadcast_to(np.asarray(inputs["L0_attn_bk"], dtype=np.float32), (128, D)))
    bv = np.ascontiguousarray(np.broadcast_to(np.asarray(inputs["L0_attn_bv"], dtype=np.float32), (128, D)))
    bo = np.ascontiguousarray(np.broadcast_to(np.asarray(inputs["L0_attn_bo"], dtype=np.float32), (128, D)))
    g_pk = bytes_of(inputs["L0_ffn_gate_W_packed"]).T
    u_pk = bytes_of(inputs["L0_ffn_up_W_packed"]).T
    d_pk = bytes_of(inputs["L0_ffn_down_W_packed"]).T
    g_sc = np.repeat(np.asarray(inputs["L0_ffn_gate_scales"], dtype=np.float32).T, 16, axis=0)
    u_sc = np.repeat(np.asarray(inputs["L0_ffn_up_scales"], dtype=np.float32).T, 16, axis=0)
    d_sc = np.repeat(np.asarray(inputs["L0_ffn_down_scales"], dtype=np.float32).T, 16, axis=0)
    ident = np.eye(128, dtype=np.float32)
    pos = np.arange(SEQ, dtype=np.float64)
    decay = np.exp(-pos / float(SEQ)).astype(np.float32)

    nc = build_nc()
    in_maps = []
    for c in range(8):
        b, hf = c // 2, c % 2
        dcol = np.ascontiguousarray(decay[hf * TOK:(hf + 1) * TOK].reshape(TOK, 1))
        drep = np.ascontiguousarray(np.broadcast_to(decay[hf * TOK:(hf + 1) * TOK], (128, TOK)))
        in_maps.append({
            "x_own": np.ascontiguousarray(x[b, hf * TOK:(hf + 1) * TOK, :]),
            "wqt": wq_t, "wkt": wk_t, "wvt": wv_t, "wot": wo_t,
            "bq_col": bq, "bk_rep": bk, "bv_rep": bv, "bo_rep": bo,
            "decay_col": dcol, "decay_rep": drep, "ident_in": ident,
            "g_pks": np.ascontiguousarray(g_pk[:, 2048 * c:2048 * (c + 1)]),
            "u_pks": np.ascontiguousarray(u_pk[:, 2048 * c:2048 * (c + 1)]),
            "d_pks": np.ascontiguousarray(d_pk[1024 * c:1024 * (c + 1), :]),
            "g_scs": np.ascontiguousarray(g_sc[:, 2048 * c:2048 * (c + 1)]),
            "u_scs": np.ascontiguousarray(u_sc[:, 2048 * c:2048 * (c + 1)]),
            "d_scs": np.ascontiguousarray(d_sc[1024 * c:1024 * (c + 1), :]),
        })
    res = run_bass_kernel_spmd(nc, in_maps, core_ids=list(range(8)))
    y = np.empty((4, SEQ, D), dtype=np.float32)
    for c in range(8):
        b, hf = c // 2, c % 2
        y[b, hf * TOK:(hf + 1) * TOK, :] = res.results[c]["y"]
    return y
